# version 09
# baseline (speedup 1.0000x reference)
import contextlib
import numpy as np
import ml_dtypes
import concourse.bass as bass
import concourse.mybir as mybir
from concourse.bass_utils import run_bass_kernel_spmd

F32 = mybir.dt.float32
BF16 = mybir.dt.bfloat16
AXX = mybir.AxisListType.X
AF = mybir.ActivationFunctionType
ALU = mybir.AluOpType
BF = ml_dtypes.bfloat16
EPS = 1e-5


def _hadamard(n):
    h = np.array([[1.0]], dtype=np.float64)
    while h.shape[0] < n:
        h = np.block([[h, h], [h, -h]])
    return h


def make_consts(H):
    NHI = H // 128
    E = 128 // NHI
    h128 = _hadamard(128).astype(BF)
    bd = np.kron(np.eye(E), _hadamard(NHI)).astype(BF)
    return h128, bd


class _Sem:
    def __init__(self, h, name):
        self.h = h
        self.name = name
        self.n = 0


class _Rec:
    def __init__(self, name):
        self.name = name
        self.ops = []
        self._hwm = {}

    def wait(self, sem, val=None):
        v = sem.n if val is None else val
        if v <= 0 or self._hwm.get(sem.name, -1) >= v:
            return
        self._hwm[sem.name] = v
        self.ops.append(("wait_ge", (sem, v), {}, []))

    def op(self, meth, *a, **k):
        entry = [meth, a, k, []]
        self.ops.append(entry)

        class _H:
            def inc(_s, sem, amt=None):
                d = 16 if meth == "dma_start" else 1
                if amt is not None:
                    d = amt
                sem.n += d
                entry[3].append((sem, d))
                return sem.n
        return _H()

    def play(self, eng):
        for meth, a, k, incs in self.ops:
            if meth == "wait_ge":
                eng.wait_ge(a[0].h, a[1])
                continue
            ins = getattr(eng, meth)(*a, **k)
            for sem, d in incs:
                ins = ins.then_inc(sem.h, d)


def _ch(total, step=512):
    return [(o, min(step, total - o)) for o in range(0, total, step)]


def build_program(T, D, H, R=64):
    DT, TT, HT = D // 128, T // 128, H // 128
    NHI = HT
    E = 128 // NHI
    assert NHI * 128 == H and E * NHI == 128
    TH = T // 2
    THT = TH // 128
    SCW = min(512, H)
    SCT = SCW // 128
    NSUP = H // SCW
    GCT = min(8, HT)
    NGRP = HT // GCT
    SUPG = GCT // SCT
    shf = NSUP // 2
    NFS = 3

    nc = bass.Bass("TRN2", target_bir_lowering=False)

    x = nc.dram_tensor("x", [T, D], F32, kind="ExternalInput").ap()
    wq = nc.dram_tensor("wq", [R, D], F32, kind="ExternalInput").ap()
    wk = nc.dram_tensor("wk", [R, D], F32, kind="ExternalInput").ap()
    wv = nc.dram_tensor("wv", [D, D], F32, kind="ExternalInput").ap()
    wo = nc.dram_tensor("wo", [D, D], F32, kind="ExternalInput").ap()
    w1 = nc.dram_tensor("w1", [H, D], F32, kind="ExternalInput").ap()
    w2 = nc.dram_tensor("w2", [D, H], F32, kind="ExternalInput").ap()
    w3 = nc.dram_tensor("w3", [H, D], F32, kind="ExternalInput").ap()
    h128c = nc.dram_tensor("h128", [128, 128], BF16, kind="ExternalInput").ap()
    bdc = nc.dram_tensor("bd", [128, 128], BF16, kind="ExternalInput").ap()
    fconst = nc.dram_tensor("fconst", [128, 2], F32, kind="ExternalInput").ap()
    out = nc.dram_tensor("out", [T, D], F32, kind="ExternalOutput").ap()

    s_wq = nc.dram_tensor("s_wq", [R, D], BF16, kind="Internal").ap()
    s_wk = nc.dram_tensor("s_wk", [R, D], BF16, kind="Internal").ap()
    s_wo = nc.dram_tensor("s_wo", [D, D], BF16, kind="Internal").ap()
    s_w1 = nc.dram_tensor("s_w1", [H, D], BF16, kind="Internal").ap()
    s_w2 = nc.dram_tensor("s_w2", [D, H], BF16, kind="Internal").ap()
    s_w3 = nc.dram_tensor("s_w3", [H, D], BF16, kind="Internal").ap()
    s_xn = nc.dram_tensor("s_xn", [T, D], BF16, kind="Internal").ap()
    s_t1 = nc.dram_tensor("s_t1", [H, D], BF16, kind="Internal").ap()
    s_w2h = nc.dram_tensor("s_w2h", [H, D], BF16, kind="Internal").ap()
    s_wvoT = nc.dram_tensor("s_wvoT", [D, D], BF16, kind="Internal").ap()
    s_u = nc.dram_tensor("s_u", [T, D], BF16, kind="Internal").ap()
    s_a = nc.dram_tensor("s_a", [T, T], BF16, kind="Internal").ap()
    s_acc = nc.dram_tensor("s_acc", [T, D], BF16, kind="Internal").ap()
    s_w1q = nc.dram_tensor("s_w1q", [NSUP * 128 * DT * SCW], BF16, kind="Internal").ap()
    s_w3q = nc.dram_tensor("s_w3q", [NSUP * 128 * DT * SCW], BF16, kind="Internal").ap()

    t1_swap = s_t1.rearrange("(b a) d -> a b d", a=NHI)
    w2h_unswap = s_w2h.rearrange("(m tau e) d -> tau e m d", m=NHI, e=E)
    w2h_rows = s_w2h.rearrange("(n p) d -> p n d", p=128)
    acc_rows = s_acc.rearrange("(n p) d -> p n d", p=128)
    w1q_sup = s_w1q.rearrange("(s p dt c) -> s p (dt c)", s=NSUP, p=128, dt=DT)
    w3q_sup = s_w3q.rearrange("(s p dt c) -> s p (dt c)", s=NSUP, p=128, dt=DT)
    w1q_wr = s_w1q.rearrange("(s p dt c) -> p dt s c", s=NSUP, p=128, dt=DT)
    w3q_wr = s_w3q.rearrange("(s p dt c) -> p dt s c", s=NSUP, p=128, dt=DT)

    sy, sc, ve, te, gp = (_Rec(n) for n in ("sy", "sc", "ve", "te", "gp"))
    stack = contextlib.ExitStack()

    def sem(name):
        return _Sem(stack.enter_context(nc.semaphore(name)), name)

    Ste, Ssc, Sve = sem("Ste"), sem("Ssc"), sem("Sve")
    Dcn = sem("Dcn")
    Dcq, Dck, Dco, Dc2 = sem("Dcq"), sem("Dck"), sem("Dco"), sem("Dc2")
    Dc1h = [sem("Dc1a"), sem("Dc1b")]
    Dc3h = [sem("Dc3a"), sem("Dc3b")]
    Dx = [sem("Dx0"), sem("Dx1")]
    Dxw = [sem("Dxw0"), sem("Dxw1")]
    Dw2t = [sem(f"Dw2t{i}") for i in range(NFS)]
    Dt1w = [sem(f"Dt1w{i}") for i in range(NFS)]
    Dw2u = [sem(f"Dw2u{i}") for i in range(NFS)]
    Dw2hw = [sem(f"Dw2hw{i}") for i in range(NFS)]
    Dwot = sem("Dwot")
    Dwv = sem("Dwv")
    Dvw = [sem("Dvw0"), sem("Dvw1")]
    DxnT = sem("DxnT")
    Dwqk = sem("Dwqk")
    Daw = [sem("Daw0"), sem("Daw1")]
    Dwvo = sem("Dwvo")
    Duw = [sem("Duw0"), sem("Duw1")]
    DxnTh = sem("DxnTh")
    Dw1 = [sem("Dw10"), sem("Dw11")]
    Dw3 = [sem("Dw30"), sem("Dw31")]
    Dw2hb = sem("Dw2hb")
    Daccw = sem("Daccw")
    Dub = sem("Dub")
    Dat = [sem("Dat0"), sem("Dat1")]
    Dxa = [sem("Dxa0"), sem("Dxa1")]
    Dow = [sem("Dow0"), sem("Dow1")]
    Dpt = [sem(f"Dpt{i}") for i in range(4)]
    Dptw = [sem(f"Dptw{i}") for i in range(4)]

    cH128 = stack.enter_context(nc.sbuf_tensor("cH128", [128, 128], BF16))
    cBD = stack.enter_context(nc.sbuf_tensor("cBD", [128, 128], BF16))
    recipb = stack.enter_context(nc.sbuf_tensor("recipb", [128, TT], F32))
    fcb = stack.enter_context(nc.sbuf_tensor("fcb", [128, 2], F32))

    ps0 = stack.enter_context(nc.psum_tensor("ps0", [128, 2048], F32))
    ps1 = stack.enter_context(nc.psum_tensor("ps1", [128, 2048], F32))
    PS = [ps0, ps1]
    ps_rel = [[], []]
    srr = [0]

    def take():
        b = srr[0] % 2
        srr[0] += 1
        for s_, v in ps_rel[b]:
            te.wait(s_, v)
        ps_rel[b] = []
        return b

    def boundary(release, recs=None):
        for rec in (recs or (sy, sc, ve, gp)):
            for s_, v in release:
                rec.wait(s_, v)

    p4t = contextlib.ExitStack()
    ptb = p4t.enter_context(nc.sbuf_tensor("ptb", [128, H], BF16, is_transpose_dest=True))
    pfold = contextlib.ExitStack()
    w2tb = pfold.enter_context(nc.sbuf_tensor("w2tb", [128, NFS * D], BF16, is_transpose_dest=True))
    foldev = pfold.enter_context(nc.sbuf_tensor("foldev", [128, NFS * D], BF16))
    p2v = contextlib.ExitStack()
    wvb = p2v.enter_context(nc.sbuf_tensor("wvb", [128, DT * D], BF16))
    wotb = p2v.enter_context(nc.sbuf_tensor("wotb", [128, DT * D], BF16, is_transpose_dest=True))
    wvoev = p2v.enter_context(nc.sbuf_tensor("wvoev", [128, 2 * D], BF16))

    sy.op("dma_start", out=cH128[:], in_=h128c).inc(Dcn)
    sy.op("dma_start", out=cBD[:], in_=bdc).inc(Dcn)
    sy.op("dma_start", out=fcb[:], in_=fconst).inc(Dcn)

    def cast(dst, src, r0, r1, nsplit, sm):
        step = (r1 - r0) // nsplit
        for rr in range(r0, r1, step):
            gp.op("dma_start", out=dst[rr:rr + step, :], in_=src[rr:rr + step, :]).inc(sm)

    cast(s_wq, wq, 0, R, 1, Dcq)
    cast(s_wk, wk, 0, R, 1, Dck)
    cast(s_wo, wo, 0, D, max(1, (D * D * 4) >> 23), Dco)

    for d1 in range(DT):
        gp.op("dma_start", out=wvb[:, d1 * D:(d1 + 1) * D],
              in_=wv[d1 * 128:(d1 + 1) * 128, :]).inc(Dwv)

    cast(s_w2, w2, 0, D, max(1, (D * H * 4) >> 23), Dc2)
    cast(s_w1, w1, 0, H // 2, max(1, (H * D * 2) >> 23), Dc1h[0])
    cast(s_w3, w3, 0, H // 2, max(1, (H * D * 2) >> 23), Dc3h[0])
    cast(s_w1, w1, H // 2, H, max(1, (H * D * 2) >> 23), Dc1h[1])
    cast(s_w3, w3, H // 2, H, max(1, (H * D * 2) >> 23), Dc3h[1])

    NST = max(1, D // 512)
    p1x = contextlib.ExitStack()
    xbuf = p1x.enter_context(nc.sbuf_tensor("xbuf", [128, 2 * D], F32))
    xnbuf = p1x.enter_context(nc.sbuf_tensor("xnbuf", [128, 2 * D], BF16))
    stat6 = p1x.enter_context(nc.sbuf_tensor("stat6", [128, 2 * 6 * NST], F32))
    mv2 = p1x.enter_context(nc.sbuf_tensor("mv2", [128, 4], F32))
    sd1 = p1x.enter_context(nc.sbuf_tensor("sd1", [128, 2], F32))
    rsb = p1x.enter_context(nc.sbuf_tensor("rsb", [128, 2], F32))
    nbb = p1x.enter_context(nc.sbuf_tensor("nbb", [128, 2], F32))

    p1_ve = [0] * TT
    p1_act = [0] * TT

    def p1_load(i):
        if i >= TT:
            return
        b = i % 2
        if i >= 2:
            sc.wait(Sve, p1_ve[i - 2])
        sc.op("dma_start", out=xbuf[:, b * D:(b + 1) * D],
              in_=x[i * 128:(i + 1) * 128, :]).inc(Dx[b])

    p1_load(0)
    p1_load(1)
    for i in range(TT):
        b = i % 2
        xs = xbuf[:, b * D:(b + 1) * D]
        ve.wait(Dx[b])
        if i >= 2:
            ve.wait(Ssc, p1_act[i - 2])
        for j in range(NST):
            w = min(512, D - j * 512)
            ve.op("bn_stats", out=stat6[:, (b * NST + j) * 6:(b * NST + j + 1) * 6],
                  in_=xs[:, j * 512:j * 512 + w]).inc(Sve)
        ve.wait(Sve)
        ve.op("bn_aggr", out=mv2[:, b * 2:b * 2 + 2],
              in_=stat6[:, b * NST * 6:(b + 1) * NST * 6]).inc(Sve)
        sc.wait(Sve)
        sc.wait(Dcn)
        sc.op("activation", out=sd1[:, b:b + 1], in_=mv2[:, b * 2 + 1:b * 2 + 2],
              func=AF.Sqrt, bias=fcb[:, 0:1]).inc(Ssc)
        ve.wait(Ssc)
        ve.op("reciprocal", out=rsb[:, b:b + 1], in_=sd1[:, b:b + 1]).inc(Sve)
        ve.wait(Sve)
        ve.op("scalar_tensor_tensor", out=nbb[:, b:b + 1], in0=mv2[:, b * 2:b * 2 + 1],
              scalar=-1.0, in1=rsb[:, b:b + 1], op0=ALU.mult, op1=ALU.mult).inc(Sve)
        p1_ve[i] = Sve.n
        sc.wait(Sve, p1_ve[i])
        sc.wait(Dx[b])
        sc.wait(Dxw[b])
        sc.op("activation", out=xnbuf[:, b * D:(b + 1) * D], in_=xs,
              func=AF.Identity, scale=rsb[:, b:b + 1], bias=nbb[:, b:b + 1]).inc(Ssc)
        p1_act[i] = Ssc.n
        sc.wait(Ssc, p1_act[i])
        sc.op("dma_start", out=s_xn[i * 128:(i + 1) * 128, :],
              in_=xnbuf[:, b * D:(b + 1) * D]).inc(Dxw[b])
        p1_load(i + 2)
    p1_sve, p1_ssc = Sve.n, Ssc.n
    p1x.close()

    p1_release = [(Sve, p1_sve), (Ssc, p1_ssc),
                  (Dx[0], Dx[0].n), (Dx[1], Dx[1].n),
                  (Dxw[0], Dxw[0].n), (Dxw[1], Dxw[1].n)]

    st1_te = [0] * HT
    st1_ev = [0] * HT
    st2_te = [0] * HT
    s1 = {"n": 0}
    s2 = {"n": 0}
    w2h_mark = []

    def pump_s1(k):
        for _ in range(k):
            a = s1["n"]
            if a >= HT:
                return
            s1["n"] += 1
            slot = a % NFS
            sy.wait(Dc2)
            if a >= NFS:
                sy.wait(Ste, st1_te[a - NFS])
            sy.op("dma_start", out=w2tb[:, slot * D:(slot + 1) * D],
                  in_=s_w2[:, a * 128:(a + 1) * 128], transpose=True).inc(Dw2t[slot])
            te.wait(Dw2t[slot])
            te.wait(Dcn)
            b = take()
            for (o, w) in _ch(D):
                st1_te[a] = te.op("matmul", PS[b][:, o:o + w], cH128[:],
                                  w2tb[:, slot * D + o:slot * D + o + w],
                                  start=True, stop=True).inc(Ste)
            ve.wait(Ste, st1_te[a])
            ve.wait(Dt1w[slot])
            sve_v = ve.op("tensor_copy", out=foldev[:, slot * D:(slot + 1) * D],
                          in_=PS[b][:, 0:D]).inc(Sve)
            st1_ev[a] = sve_v
            ps_rel[b] = [(Sve, sve_v)]
            sc.wait(Sve, sve_v)
            sc.op("dma_start", out=t1_swap[a],
                  in_=foldev[:, slot * D:(slot + 1) * D]).inc(Dt1w[slot])

    def pump_s2(k):
        for _ in range(k):
            t = s2["n"]
            if t >= HT:
                return
            s2["n"] += 1
            slot = t % NFS
            if t == 0:
                assert s1["n"] == HT
                for s_ in Dt1w:
                    gp.wait(s_)
                    sc.wait(s_)
                gp.wait(Ste, max(st1_te))
                sc.wait(Sve, max(st1_ev))
            if t >= NFS:
                gp.wait(Ste, st2_te[t - NFS])
            gp.op("dma_start", out=w2tb[:, slot * D:(slot + 1) * D],
                  in_=s_t1[t * 128:(t + 1) * 128, :]).inc(Dw2u[slot])
            te.wait(Dw2u[slot])
            b = take()
            for (o, w) in _ch(D):
                st2_te[t] = te.op("matmul", PS[b][:, o:o + w], cBD[:],
                                  w2tb[:, slot * D + o:slot * D + o + w],
                                  start=True, stop=True).inc(Ste)
            sc.wait(Ste, st2_te[t])
            sc.wait(Dw2hw[slot])
            ssc_v = sc.op("activation", out=foldev[:, slot * D:(slot + 1) * D],
                          in_=PS[b][:, 0:D], func=AF.Identity,
                          scale=fcb[:, 1:2]).inc(Ssc)
            ps_rel[b] = [(Ssc, ssc_v)]
            gp.wait(Ssc, ssc_v)
            gp.op("dma_start", out=w2h_unswap[t],
                  in_=foldev[:, slot * D:(slot + 1) * D]).inc(Dw2hw[slot])
            if s2["n"] % GCT == 0:
                w2h_mark.append([(s_, s_.n) for s_ in Dw2hw])

    sy.wait(Dco)
    for d1 in range(DT):
        sy.op("dma_start", out=wotb[:, d1 * D:(d1 + 1) * D],
              in_=s_wo[:, d1 * 128:(d1 + 1) * 128], transpose=True).inc(Dwot)

    te.wait(Dwot)
    te.wait(Dwv)
    wvo_te = [0] * DT
    for d2t in range(DT):
        b = take()
        for d1 in range(DT):
            lhs = wotb[:, d1 * D + d2t * 128: d1 * D + (d2t + 1) * 128]
            for (o, w) in _ch(D):
                hh = te.op("matmul", PS[b][:, o:o + w], lhs,
                           wvb[:, d1 * D + o:d1 * D + o + w],
                           start=(d1 == 0), stop=(d1 == DT - 1))
                if d1 == DT - 1:
                    wvo_te[d2t] = hh.inc(Ste)
        ve.wait(Ste, wvo_te[d2t])
        if d2t >= 2:
            ve.wait(Dvw[d2t % 2])
        sve_v = ve.op("tensor_copy", out=wvoev[:, (d2t % 2) * D:(d2t % 2 + 1) * D],
                      in_=PS[b][:, 0:D]).inc(Sve)
        ps_rel[b] = [(Sve, sve_v)]
        gp.wait(Sve, sve_v)
        gp.op("dma_start", out=s_wvoT[d2t * 128:(d2t + 1) * 128, :],
              in_=wvoev[:, (d2t % 2) * D:(d2t % 2 + 1) * D]).inc(Dvw[d2t % 2])
    p3_te_end = Ste.n
    p3_ve_end = Sve.n

    p2v_release = [(Ste, p3_te_end), (Sve, p3_ve_end),
                   (Dvw[0], Dvw[0].n), (Dvw[1], Dvw[1].n), (Dwv, Dwv.n)]
    p2v.close()

    p46 = contextlib.ExitStack()
    xnT = p46.enter_context(nc.sbuf_tensor("xnT", [128, DT * T], BF16, is_transpose_dest=True))
    wqkT = p46.enter_context(nc.sbuf_tensor("wqkT", [128, 2 * DT * R], BF16, is_transpose_dest=True))
    qfT = p46.enter_context(nc.sbuf_tensor("qfT", [R, T], BF16))
    kfT = p46.enter_context(nc.sbuf_tensor("kfT", [R, T], BF16))

    boundary(p2v_release + p1_release)

    sy.wait(Dxw[0])
    sy.wait(Dxw[1])
    for d in range(DT):
        sy.op("dma_start", out=xnT[:, d * T:(d + 1) * T],
              in_=s_xn[:, d * 128:(d + 1) * 128], transpose=True).inc(DxnT)
    sy.wait(Dcq)
    sy.wait(Dck)
    for d in range(DT):
        sy.op("dma_start", out=wqkT[:, d * R:(d + 1) * R],
              in_=s_wq[:, d * 128:(d + 1) * 128], transpose=True).inc(Dwqk)
        sy.op("dma_start", out=wqkT[:, (DT + d) * R:(DT + d + 1) * R],
              in_=s_wk[:, d * 128:(d + 1) * 128], transpose=True).inc(Dwqk)
    wvob = p46.enter_context(nc.sbuf_tensor("wvob", [128, DT * D], BF16, is_transpose_dest=True))
    sy.wait(Dvw[0])
    sy.wait(Dvw[1])
    for d in range(DT):
        sy.op("dma_start", out=wvob[:, d * D:(d + 1) * D],
              in_=s_wvoT[:, d * 128:(d + 1) * 128], transpose=True).inc(Dwvo)

    te.wait(DxnT)
    te.wait(Dwqk)
    for (o, w) in _ch(T):
        b0 = take()
        b1 = take()
        for d in range(DT):
            h0 = te.op("matmul", PS[b0][0:R, o:o + w], wqkT[:, d * R:(d + 1) * R],
                       xnT[:, d * T + o:d * T + o + w], start=(d == 0), stop=(d == DT - 1))
        h0.inc(Ste)
        for d in range(DT):
            h1 = te.op("matmul", PS[b1][0:R, o:o + w], wqkT[:, (DT + d) * R:(DT + d + 1) * R],
                       xnT[:, d * T + o:d * T + o + w], start=(d == 0), stop=(d == DT - 1))
        qk_te = h1.inc(Ste)
        sc.wait(Ste, qk_te)
        sc.op("activation", out=qfT[:, o:o + w], in_=PS[b0][0:R, o:o + w], func=AF.Copy).inc(Ssc)
        qk_sc = sc.op("activation", out=kfT[:, o:o + w], in_=PS[b1][0:R, o:o + w],
                      func=AF.Copy).inc(Ssc)
        ps_rel[b0] = [(Ssc, qk_sc)]
        ps_rel[b1] = [(Ssc, qk_sc)]
    qk_done_sc = Ssc.n

    abuf = p46.enter_context(nc.sbuf_tensor("abuf", [128, 2 * T], BF16))
    negm = p46.enter_context(nc.sbuf_tensor("negm", [128, 2], F32))
    sumb = p46.enter_context(nc.sbuf_tensor("sumb", [128, 2], F32))
    uev = p46.enter_context(nc.sbuf_tensor("uev", [128, 2 * D], BF16))

    te.wait(Ssc, qk_done_sc)
    a_sc = [0] * TT
    for i in range(TT):
        bs = i % 2
        b = take()
        s_te = 0
        for (o, w) in _ch(T):
            s_te = te.op("matmul", PS[b][:, o:o + w], qfT[:, i * 128:(i + 1) * 128],
                         kfT[:, o:o + w], start=True, stop=True).inc(Ste)
        ve.wait(Ste, s_te)
        if i >= 2:
            ve.wait(Ssc, a_sc[i - 2])
        negm_v = ve.op("reduce_max", out=negm[:, bs:bs + 1], in_=PS[b][:, 0:T], axis=AXX,
                       negate=True).inc(Sve)
        sc.wait(Sve, negm_v)
        sc.wait(Daw[bs])
        a_sc[i] = sc.op("activation", out=abuf[:, bs * T:(bs + 1) * T], in_=PS[b][:, 0:T],
                        func=AF.Exp, bias=negm[:, bs:bs + 1]).inc(Ssc)
        ps_rel[b] = [(Ssc, a_sc[i]), (Sve, negm_v)]
        ve.wait(Ssc, a_sc[i])
        ve.op("reduce_sum", out=sumb[:, bs:bs + 1], in_=abuf[:, bs * T:(bs + 1) * T],
              axis=AXX).inc(Sve)
        ve.wait(Sve)
        ve.op("reciprocal", out=recipb[:, i:i + 1], in_=sumb[:, bs:bs + 1]).inc(Sve)
        gp.wait(Ssc, a_sc[i])
        gp.op("dma_start", out=s_a[i * 128:(i + 1) * 128, :],
              in_=abuf[:, bs * T:(bs + 1) * T]).inc(Daw[bs])
        pump_s1(max(2, (HT + TT - 1) // TT))
    pump_s1(HT)
    recip_done = Sve.n

    te.wait(Dwvo)
    for i in range(TT):
        bs = i % 2
        b = take()
        u_te = 0
        for d in range(DT):
            lhs = xnT[:, d * T + i * 128: d * T + (i + 1) * 128]
            for (o, w) in _ch(D):
                hh = te.op("matmul", PS[b][:, o:o + w], lhs,
                           wvob[:, d * D + o:d * D + o + w],
                           start=(d == 0), stop=(d == DT - 1))
                if d == DT - 1:
                    u_te = hh.inc(Ste)
        sc.wait(Ste, u_te)
        sc.wait(Duw[bs])
        ssc_v = sc.op("activation", out=uev[:, bs * D:(bs + 1) * D], in_=PS[b][:, 0:D],
                      func=AF.Copy).inc(Ssc)
        ps_rel[b] = [(Ssc, ssc_v)]
        gp.wait(Ssc, ssc_v)
        gp.op("dma_start", out=s_u[i * 128:(i + 1) * 128, :],
              in_=uev[:, bs * D:(bs + 1) * D]).inc(Duw[bs])
        pump_s2(3)
    recip_done = recip_done

    HH = H // 2
    pret_half_mark = [None, None]
    pret_state = {"it": 0}

    def pump_pret(k):
        for _ in range(k):
            it = pret_state["it"]
            if it >= 4 * DT:
                return
            pret_state["it"] += 1
            hf = it // (2 * DT)
            rem = it % (2 * DT)
            d = rem // 2
            (s_src, d_wr, dcast) = [(s_w1, w1q_wr, Dc1h[hf]), (s_w3, w3q_wr, Dc3h[hf])][rem % 2]
            slot = 2 * hf + (it % 2)
            sy.wait(dcast)
            if hf == 1:
                sy.wait(Dptw[slot - 2])
            sy.wait(Dptw[slot])
            sy.op("dma_start", out=ptb[:, (slot % 2) * HH:(slot % 2 + 1) * HH],
                  in_=s_src[hf * HH:(hf + 1) * HH, d * 128:(d + 1) * 128],
                  transpose=True).inc(Dpt[slot])
            gp.wait(Dpt[slot])
            gp.op("dma_start", out=d_wr[:, d, hf * shf:(hf + 1) * shf, :],
                  in_=ptb[:, (slot % 2) * HH:(slot % 2 + 1) * HH]).inc(Dptw[slot])
            if pret_state["it"] == 2 * DT:
                pret_half_mark[0] = (Dptw[0].n, Dptw[1].n)
            if pret_state["it"] == 4 * DT:
                pret_half_mark[1] = (Dptw[2].n, Dptw[3].n)

    for _ in range(2 * DT):
        pump_pret(1)
        pump_s2(1)
    pump_s2(HT)
    assert s2["n"] == HT and len(w2h_mark) == NGRP

    p46_te_end = Ste.n
    p46.close()
    pfold_release = [(Ste, max(max(st2_te), 1)),
                     *[(s_, s_.n) for s_ in Dw2u], *[(s_, s_.n) for s_ in Dw2hw]]
    pfold.close()

    p7 = contextlib.ExitStack()
    xnTh = p7.enter_context(nc.sbuf_tensor("xnTh", [128, DT * TH], BF16, is_transpose_dest=True))
    acch = p7.enter_context(nc.sbuf_tensor("acch", [128, THT * D], BF16))
    gbuf = p7.enter_context(nc.sbuf_tensor("gbuf", [128, GCT * TH], BF16))
    w1b = p7.enter_context(nc.sbuf_tensor("w1b", [128, 2 * DT * SCW], BF16))
    w3b = p7.enter_context(nc.sbuf_tensor("w3b", [128, 2 * DT * SCW], BF16))
    w2hb = p7.enter_context(nc.sbuf_tensor("w2hb", [128, GCT * D], BF16))
    silt = p7.enter_context(nc.sbuf_tensor("silt", [128, 2 * TH], BF16))

    boundary([(Ste, p46_te_end), (Sve, Sve.n), (Ssc, Ssc.n),
              (Daw[0], Daw[0].n), (Daw[1], Daw[1].n),
              (Duw[0], Duw[0].n), (Duw[1], Duw[1].n),
              *pfold_release])

    sil_guard = [0, 0]
    acc_spill = [0, 0]
    for half in range(2):
        r0h = half * TH
        for d in range(DT):
            sy.op("dma_start", out=xnTh[:, d * TH:(d + 1) * TH],
                  in_=s_xn[r0h:r0h + TH, d * 128:(d + 1) * 128], transpose=True).inc(DxnTh)
        xnTh_done = DxnTh.n

        sup_te = [0] * NSUP

        def p7_load_super(s, _sup_te=sup_te):
            if s >= NSUP:
                return
            pump_pret(4 * DT)if pret_half_mark[s // shf] is None else None
            bsl = s % 2
            hf = s // shf
            m0, m1 = pret_half_mark[hf]
            sy.wait(Dptw[2 * hf], m0)
            sy.wait(Dptw[2 * hf + 1], m1)
            gp.wait(Dptw[2 * hf], m0)
            gp.wait(Dptw[2 * hf + 1], m1)
            if s >= 2:
                sy.wait(Ste, _sup_te[s - 2])
                gp.wait(Ste, _sup_te[s - 2])
            sy.op("dma_start", out=w1b[:, bsl * DT * SCW:(bsl + 1) * DT * SCW],
                  in_=w1q_sup[s]).inc(Dw1[bsl])
            gp.op("dma_start", out=w3b[:, bsl * DT * SCW:(bsl + 1) * DT * SCW],
                  in_=w3q_sup[s]).inc(Dw3[bsl])

        p7_load_super(0)
        p7_load_super(1)

        te.wait(DxnTh, xnTh_done)
        acc_ve = [0] * THT
        for g in range(NGRP):
            gp.wait(Ste)
            for s_, v in w2h_mark[g]:
                gp.wait(s_, v)
            gp.op("dma_start", out=w2hb[:, 0:GCT * D],
                  in_=w2h_rows[:, g * GCT:(g + 1) * GCT, :]).inc(Dw2hb)

            g_mul = 0
            for si in range(SUPG):
                s = g * SUPG + si
                bsl = s % 2
                te.wait(Dw1[bsl])
                te.wait(Dw3[bsl])
                for ct in range(SCT):
                    c_glob = s * SCT + ct
                    cg = c_glob - g * GCT
                    slot = c_glob % 2
                    pb = take()
                    gate_te = lin_te = 0
                    for d in range(DT):
                        lhs = w1b[:, (bsl * DT + d) * SCW + ct * 128:(bsl * DT + d) * SCW + (ct + 1) * 128]
                        for (o, w) in _ch(TH):
                            hh = te.op("matmul", PS[pb][:, o:o + w], lhs,
                                       xnTh[:, d * TH + o:d * TH + o + w],
                                       start=(d == 0), stop=(d == DT - 1))
                            if d == DT - 1:
                                gate_te = hh.inc(Ste)
                    for d in range(DT):
                        lhs = w3b[:, (bsl * DT + d) * SCW + ct * 128:(bsl * DT + d) * SCW + (ct + 1) * 128]
                        for (o, w) in _ch(TH):
                            hh = te.op("matmul", PS[pb][:, 1024 + o:1024 + o + w], lhs,
                                       xnTh[:, d * TH + o:d * TH + o + w],
                                       start=(d == 0), stop=(d == DT - 1))
                            if d == DT - 1:
                                lin_te = hh.inc(Ste)
                    sc.wait(Ste, gate_te)
                    sc.wait(Sve, sil_guard[slot])
                    sil_sc = sc.op("activation", out=silt[:, slot * TH:(slot + 1) * TH],
                                   in_=PS[pb][:, 0:TH], func=AF.Silu).inc(Ssc)
                    ve.wait(Ssc, sil_sc)
                    ve.wait(Ste, lin_te)
                    g_mul = ve.op("tensor_tensor", out=gbuf[:, cg * TH:(cg + 1) * TH],
                                  in0=silt[:, slot * TH:(slot + 1) * TH],
                                  in1=PS[pb][:, 1024:1024 + TH], op=ALU.mult).inc(Sve)
                    sil_guard[slot] = g_mul
                    ps_rel[pb] = [(Sve, g_mul), (Ssc, sil_sc)]
                sup_te[s] = Ste.n
                if half == 0:
                    pump_pret(2 + 4 * DT // max(1, NSUP))
                p7_load_super(s + 2)
            te.wait(Sve, g_mul)
            te.wait(Dw2hb)
            for j in range(THT):
                pb = take()
                dn_te = 0
                for ci in range(GCT):
                    lhs = gbuf[:, ci * TH + j * 128: ci * TH + (j + 1) * 128]
                    for (o, w) in _ch(D):
                        hh = te.op("matmul", PS[pb][:, o:o + w], lhs,
                                   w2hb[:, ci * D + o:ci * D + o + w],
                                   start=(ci == 0), stop=(ci == GCT - 1))
                        if ci == GCT - 1:
                            dn_te = hh.inc(Ste)
                ve.wait(Ste, dn_te)
                if g == 0:
                    if half == 1 and j == 0:
                        ve.wait(Daccw, acc_spill[0])
                    hh = ve.op("tensor_copy", out=acch[:, j * D:(j + 1) * D], in_=PS[pb][:, 0:D])
                else:
                    ve.wait(Sve, acc_ve[j])
                    hh = ve.op("tensor_tensor", out=acch[:, j * D:(j + 1) * D],
                               in0=PS[pb][:, 0:D], in1=acch[:, j * D:(j + 1) * D], op=ALU.add)
                acc_ve[j] = hh.inc(Sve)
                ps_rel[pb] = [(Sve, acc_ve[j])]
            if g + 1 < NGRP:
                ve.wait(Ste, Ste.n)
        for j in range(THT):
            gp.wait(Sve, acc_ve[j])
        gp.op("dma_start", out=acc_rows[:, half * THT:(half + 1) * THT, :],
              in_=acch[:, 0:THT * D]).inc(Daccw)
        acc_spill[half] = Daccw.n
        if half == 0:
            sy.wait(Ste, Ste.n)
            ve.wait(Ste, Ste.n)
    p7_te_end = Ste.n
    p7.close()
    p4t_release = [*[(s_, s_.n) for s_ in Dpt], *[(s_, s_.n) for s_ in Dptw]]
    p4t.close()

    p8 = contextlib.ExitStack()
    ubuf = p8.enter_context(nc.sbuf_tensor("ubuf", [128, DT * D], BF16))
    NQC = max(1, T // 512)
    QW = T // NQC
    QT = QW // 128
    atb = p8.enter_context(nc.sbuf_tensor("atb", [128, 2 * TT * QW], BF16, is_transpose_dest=True))
    xrb = p8.enter_context(nc.sbuf_tensor("xrb", [128, 2 * D], F32))
    accrb = p8.enter_context(nc.sbuf_tensor("accrb", [128, 2 * D], BF16))
    t2b = p8.enter_context(nc.sbuf_tensor("t2b", [128, 2 * D], F32))
    outb = p8.enter_context(nc.sbuf_tensor("outb", [128, 2 * D], F32))

    boundary([(Ste, p7_te_end), (Sve, Sve.n), (Ssc, Ssc.n), (Daccw, Daccw.n),
              *p4t_release])
    gp.wait(Duw[0])
    gp.wait(Duw[1])
    for tk in range(TT):
        gp.op("dma_start", out=ubuf[:, tk * D:(tk + 1) * D],
              in_=s_u[tk * 128:(tk + 1) * 128, :]).inc(Dub)

    at_te = [0] * NQC
    sy.wait(Daw[0])
    sy.wait(Daw[1])
    ve.wait(Sve, recip_done)

    def p8_load_at(q):
        if q >= NQC:
            return
        bq = q % 2
        if q >= 2:
            sy.wait(Ste, at_te[q - 2])
        for tk in range(TT):
            sy.op("dma_start", out=atb[:, (bq * TT + tk) * QW:(bq * TT + tk + 1) * QW],
                  in_=s_a[q * QW:(q + 1) * QW, tk * 128:(tk + 1) * 128],
                  transpose=True).inc(Dat[bq])

    p8_load_at(0)
    p8_load_at(1)
    te.wait(Dub)
    out_ve = [0] * TT

    def p8_load_xacc(i):
        if i >= TT:
            return
        bs = i % 2
        if i >= 2:
            sc.wait(Sve, out_ve[i - 2])
        sc.op("dma_start", out=xrb[:, bs * D:(bs + 1) * D],
              in_=x[i * 128:(i + 1) * 128, :]).inc(Dxa[bs])
        if i == 0:
            sc.wait(Daccw)
        sc.op("dma_start", out=accrb[:, bs * D:(bs + 1) * D],
              in_=s_acc[i * 128:(i + 1) * 128, :]).inc(Dxa[bs])

    p8_load_xacc(0)
    p8_load_xacc(1)
    for q in range(NQC):
        bq = q % 2
        te.wait(Dat[bq])
        for j in range(QT):
            i = q * QT + j
            bs = i % 2
            b = take()
            a_te = 0
            for tk in range(TT):
                lhs = atb[:, (bq * TT + tk) * QW + j * 128:(bq * TT + tk) * QW + (j + 1) * 128]
                for (o, w) in _ch(D):
                    hh = te.op("matmul", PS[b][:, o:o + w], lhs,
                               ubuf[:, tk * D + o:tk * D + o + w],
                               start=(tk == 0), stop=(tk == TT - 1))
                    if tk == TT - 1:
                        a_te = hh.inc(Ste)
            if j == QT - 1:
                at_te[q] = a_te
            ve.wait(Ste, a_te)
            ve.wait(Dxa[bs])
            ve.wait(Dow[bs])
            ve.op("tensor_tensor", out=t2b[:, bs * D:(bs + 1) * D],
                  in0=xrb[:, bs * D:(bs + 1) * D], in1=accrb[:, bs * D:(bs + 1) * D],
                  op=ALU.add).inc(Sve)
            ve.wait(Sve)
            out_ve[i] = ve.op("scalar_tensor_tensor", out=outb[:, bs * D:(bs + 1) * D],
                              in0=PS[b][:, 0:D], scalar=recipb[:, i:i + 1],
                              in1=t2b[:, bs * D:(bs + 1) * D],
                              op0=ALU.mult, op1=ALU.add).inc(Sve)
            ps_rel[b] = [(Sve, out_ve[i])]
            gp.wait(Sve, out_ve[i])
            gp.op("dma_start", out=out[i * 128:(i + 1) * 128, :],
                  in_=outb[:, bs * D:(bs + 1) * D]).inc(Dow[bs])
            p8_load_xacc(i + 2)
        p8_load_at(q + 2)
    gp.wait(Dow[0])
    gp.wait(Dow[1])
    p8.close()

    with nc.Block() as block:
        @block.sync
        def _(eng):
            sy.play(eng)

        @block.scalar
        def _(eng):
            sc.play(eng)

        @block.vector
        def _(eng):
            ve.play(eng)

        @block.tensor
        def _(eng):
            te.play(eng)

        @block.gpsimd
        def _(eng):
            gp.play(eng)

    stack.close()
    return nc


def kernel(L0_attn_hidden_states, L0_attn_Wq, L0_attn_Wk, L0_attn_Wv, L0_attn_Wo,
           L0_ffn_W1, L0_ffn_W2, L0_ffn_W3):
    x = np.ascontiguousarray(np.asarray(L0_attn_hidden_states, dtype=np.float32))
    B, T, D = x.shape
    H = L0_ffn_W1.shape[0]
    R = L0_attn_Wq.shape[0]
    nc = build_program(T, D, H, R)
    h128, bd = make_consts(H)
    common = {
        "wq": np.ascontiguousarray(np.asarray(L0_attn_Wq, np.float32)),
        "wk": np.ascontiguousarray(np.asarray(L0_attn_Wk, np.float32)),
        "wv": np.ascontiguousarray(np.asarray(L0_attn_Wv, np.float32)),
        "wo": np.ascontiguousarray(np.asarray(L0_attn_Wo, np.float32)),
        "w1": np.ascontiguousarray(np.asarray(L0_ffn_W1, np.float32)),
        "w2": np.ascontiguousarray(np.asarray(L0_ffn_W2, np.float32)),
        "w3": np.ascontiguousarray(np.asarray(L0_ffn_W3, np.float32)),
        "h128": h128,
        "bd": bd,
        "fconst": np.tile(np.array([[EPS, 1.0 / np.sqrt(H)]], np.float32), (128, 1)),
    }
    in_maps = [{"x": np.ascontiguousarray(x[i]), **common} for i in range(B)]
    res = run_bass_kernel_spmd(nc, in_maps, core_ids=list(range(B)))
    return np.stack([np.asarray(r["out"], dtype=np.float32) for r in res.results], axis=0)


# version 10
# speedup vs baseline: 1.0014x; 1.0014x over previous
import contextlib
import numpy as np
import ml_dtypes
import concourse.bass as bass
import concourse.mybir as mybir
from concourse.bass_utils import run_bass_kernel_spmd

F32 = mybir.dt.float32
BF16 = mybir.dt.bfloat16
AXX = mybir.AxisListType.X
AF = mybir.ActivationFunctionType
ALU = mybir.AluOpType
BF = ml_dtypes.bfloat16
EPS = 1e-5


def _hadamard(n):
    h = np.array([[1.0]], dtype=np.float64)
    while h.shape[0] < n:
        h = np.block([[h, h], [h, -h]])
    return h


def make_consts(H):
    NHI = H // 128
    E = 128 // NHI
    h128 = _hadamard(128).astype(BF)
    bd = np.kron(np.eye(E), _hadamard(NHI)).astype(BF)
    return h128, bd


class _Sem:
    def __init__(self, h, name):
        self.h = h
        self.name = name
        self.n = 0


class _Rec:
    def __init__(self, name):
        self.name = name
        self.ops = []
        self._hwm = {}

    def wait(self, sem, val=None):
        v = sem.n if val is None else val
        if v <= 0 or self._hwm.get(sem.name, -1) >= v:
            return
        self._hwm[sem.name] = v
        self.ops.append(("wait_ge", (sem, v), {}, []))

    def op(self, meth, *a, **k):
        entry = [meth, a, k, []]
        self.ops.append(entry)

        class _H:
            def inc(_s, sem, amt=None):
                d = 16 if meth == "dma_start" else 1
                if amt is not None:
                    d = amt
                sem.n += d
                entry[3].append((sem, d))
                return sem.n
        return _H()

    def play(self, eng):
        for meth, a, k, incs in self.ops:
            if meth == "wait_ge":
                eng.wait_ge(a[0].h, a[1])
                continue
            ins = getattr(eng, meth)(*a, **k)
            for sem, d in incs:
                ins = ins.then_inc(sem.h, d)


def _ch(total, step=512):
    return [(o, min(step, total - o)) for o in range(0, total, step)]


def build_program(T, D, H, R=64):
    DT, TT, HT = D // 128, T // 128, H // 128
    NHI = HT
    E = 128 // NHI
    assert NHI * 128 == H and E * NHI == 128
    TH = T // 2
    THT = TH // 128
    SCW = min(512, H)
    SCT = SCW // 128
    NSUP = H // SCW
    GCT = min(8, HT)
    NGRP = HT // GCT
    SUPG = GCT // SCT
    shf = NSUP // 2
    NFS = 4

    nc = bass.Bass("TRN2", target_bir_lowering=False)

    x = nc.dram_tensor("x", [T, D], F32, kind="ExternalInput").ap()
    wq = nc.dram_tensor("wq", [R, D], F32, kind="ExternalInput").ap()
    wk = nc.dram_tensor("wk", [R, D], F32, kind="ExternalInput").ap()
    wv = nc.dram_tensor("wv", [D, D], F32, kind="ExternalInput").ap()
    wo = nc.dram_tensor("wo", [D, D], F32, kind="ExternalInput").ap()
    w1 = nc.dram_tensor("w1", [H, D], F32, kind="ExternalInput").ap()
    w2 = nc.dram_tensor("w2", [D, H], F32, kind="ExternalInput").ap()
    w3 = nc.dram_tensor("w3", [H, D], F32, kind="ExternalInput").ap()
    h128c = nc.dram_tensor("h128", [128, 128], BF16, kind="ExternalInput").ap()
    bdc = nc.dram_tensor("bd", [128, 128], BF16, kind="ExternalInput").ap()
    fconst = nc.dram_tensor("fconst", [128, 2], F32, kind="ExternalInput").ap()
    out = nc.dram_tensor("out", [T, D], F32, kind="ExternalOutput").ap()

    s_wq = nc.dram_tensor("s_wq", [R, D], BF16, kind="Internal").ap()
    s_wk = nc.dram_tensor("s_wk", [R, D], BF16, kind="Internal").ap()
    s_wo = nc.dram_tensor("s_wo", [D, D], BF16, kind="Internal").ap()
    s_w1 = nc.dram_tensor("s_w1", [H, D], BF16, kind="Internal").ap()
    s_w2 = nc.dram_tensor("s_w2", [D, H], BF16, kind="Internal").ap()
    s_w3 = nc.dram_tensor("s_w3", [H, D], BF16, kind="Internal").ap()
    s_xn = nc.dram_tensor("s_xn", [T, D], BF16, kind="Internal").ap()
    s_t1 = nc.dram_tensor("s_t1", [H, D], BF16, kind="Internal").ap()
    s_w2h = nc.dram_tensor("s_w2h", [H, D], BF16, kind="Internal").ap()
    s_wvoT = nc.dram_tensor("s_wvoT", [D, D], BF16, kind="Internal").ap()
    s_u = nc.dram_tensor("s_u", [T, D], BF16, kind="Internal").ap()
    s_a = nc.dram_tensor("s_a", [T, T], BF16, kind="Internal").ap()
    s_acc = nc.dram_tensor("s_acc", [T, D], BF16, kind="Internal").ap()
    s_w1q = nc.dram_tensor("s_w1q", [NSUP * 128 * DT * SCW], BF16, kind="Internal").ap()
    s_w3q = nc.dram_tensor("s_w3q", [NSUP * 128 * DT * SCW], BF16, kind="Internal").ap()

    t1_swap = s_t1.rearrange("(b a) d -> a b d", a=NHI)
    w2h_unswap = s_w2h.rearrange("(m tau e) d -> tau e m d", m=NHI, e=E)
    w2h_rows = s_w2h.rearrange("(n p) d -> p n d", p=128)
    acc_rows = s_acc.rearrange("(n p) d -> p n d", p=128)
    w1q_sup = s_w1q.rearrange("(s p dt c) -> s p (dt c)", s=NSUP, p=128, dt=DT)
    w3q_sup = s_w3q.rearrange("(s p dt c) -> s p (dt c)", s=NSUP, p=128, dt=DT)
    w1q_wr = s_w1q.rearrange("(s p dt c) -> p dt s c", s=NSUP, p=128, dt=DT)
    w3q_wr = s_w3q.rearrange("(s p dt c) -> p dt s c", s=NSUP, p=128, dt=DT)

    sy, sc, ve, te, gp = (_Rec(n) for n in ("sy", "sc", "ve", "te", "gp"))
    stack = contextlib.ExitStack()

    def sem(name):
        return _Sem(stack.enter_context(nc.semaphore(name)), name)

    Ste, Ssc, Sve = sem("Ste"), sem("Ssc"), sem("Sve")
    Dcn = sem("Dcn")
    Dcq, Dck, Dco, Dc2 = sem("Dcq"), sem("Dck"), sem("Dco"), sem("Dc2")
    Dc1h = [sem("Dc1a"), sem("Dc1b")]
    Dc3h = [sem("Dc3a"), sem("Dc3b")]
    Dx = [sem("Dx0"), sem("Dx1")]
    Dxw = [sem("Dxw0"), sem("Dxw1")]
    Dw2t = [sem(f"Dw2t{i}") for i in range(NFS)]
    Dt1w = [sem(f"Dt1w{i}") for i in range(NFS)]
    Dw2u = [sem(f"Dw2u{i}") for i in range(NFS)]
    Dw2hw = [sem(f"Dw2hw{i}") for i in range(NFS)]
    Dwot = sem("Dwot")
    Dwv = sem("Dwv")
    Dvw = [sem("Dvw0"), sem("Dvw1")]
    DxnT = sem("DxnT")
    Dwqk = sem("Dwqk")
    Daw = [sem("Daw0"), sem("Daw1")]
    Dwvo = sem("Dwvo")
    Duw = [sem("Duw0"), sem("Duw1"), sem("Duw2")]
    DxnTh = sem("DxnTh")
    Dw1 = [sem("Dw10"), sem("Dw11")]
    Dw3 = [sem("Dw30"), sem("Dw31")]
    Dw2hb = sem("Dw2hb")
    Daccw = sem("Daccw")
    Dub = sem("Dub")
    Dat = [sem("Dat0"), sem("Dat1")]
    Dxa = [sem("Dxa0"), sem("Dxa1")]
    Dow = [sem("Dow0"), sem("Dow1")]
    Dpt = [sem(f"Dpt{i}") for i in range(4)]
    Dptw = [sem(f"Dptw{i}") for i in range(4)]

    cH128 = stack.enter_context(nc.sbuf_tensor("cH128", [128, 128], BF16))
    cBD = stack.enter_context(nc.sbuf_tensor("cBD", [128, 128], BF16))
    recipb = stack.enter_context(nc.sbuf_tensor("recipb", [128, TT], F32))
    fcb = stack.enter_context(nc.sbuf_tensor("fcb", [128, 2], F32))

    ps0 = stack.enter_context(nc.psum_tensor("ps0", [128, 2048], F32))
    ps1 = stack.enter_context(nc.psum_tensor("ps1", [128, 2048], F32))
    PS = [ps0, ps1]
    ps_rel = [[], []]
    srr = [0]

    def take():
        b = srr[0] % 2
        srr[0] += 1
        for s_, v in ps_rel[b]:
            te.wait(s_, v)
        ps_rel[b] = []
        return b

    def boundary(release, recs=None):
        for rec in (recs or (sy, sc, ve, gp)):
            for s_, v in release:
                rec.wait(s_, v)

    p4t = contextlib.ExitStack()
    ptb = p4t.enter_context(nc.sbuf_tensor("ptb", [128, H], BF16, is_transpose_dest=True))
    pfold = contextlib.ExitStack()
    w2tb = pfold.enter_context(nc.sbuf_tensor("w2tb", [128, NFS * D], BF16, is_transpose_dest=True))
    foldev = pfold.enter_context(nc.sbuf_tensor("foldev", [128, NFS * D], BF16))
    p2v = contextlib.ExitStack()
    wvb = p2v.enter_context(nc.sbuf_tensor("wvb", [128, DT * D], BF16))
    wotb = p2v.enter_context(nc.sbuf_tensor("wotb", [128, DT * D], BF16, is_transpose_dest=True))
    wvoev = p2v.enter_context(nc.sbuf_tensor("wvoev", [128, D], BF16))

    sy.op("dma_start", out=cH128[:], in_=h128c).inc(Dcn)
    sy.op("dma_start", out=cBD[:], in_=bdc).inc(Dcn)
    sy.op("dma_start", out=fcb[:], in_=fconst).inc(Dcn)

    def cast(dst, src, r0, r1, nsplit, sm):
        step = (r1 - r0) // nsplit
        for rr in range(r0, r1, step):
            gp.op("dma_start", out=dst[rr:rr + step, :], in_=src[rr:rr + step, :]).inc(sm)

    cast(s_wq, wq, 0, R, 1, Dcq)
    cast(s_wk, wk, 0, R, 1, Dck)
    cast(s_wo, wo, 0, D, max(1, (D * D * 4) >> 23), Dco)

    for d1 in range(DT):
        gp.op("dma_start", out=wvb[:, d1 * D:(d1 + 1) * D],
              in_=wv[d1 * 128:(d1 + 1) * 128, :]).inc(Dwv)

    cast(s_w2, w2, 0, D, max(1, (D * H * 4) >> 23), Dc2)
    cast(s_w1, w1, 0, H // 2, max(1, (H * D * 2) >> 23), Dc1h[0])
    cast(s_w3, w3, 0, H // 2, max(1, (H * D * 2) >> 23), Dc3h[0])
    cast(s_w1, w1, H // 2, H, max(1, (H * D * 2) >> 23), Dc1h[1])
    cast(s_w3, w3, H // 2, H, max(1, (H * D * 2) >> 23), Dc3h[1])

    NST = max(1, D // 512)
    p1x = contextlib.ExitStack()
    xbuf = p1x.enter_context(nc.sbuf_tensor("xbuf", [128, 2 * D], F32))
    xnbuf = p1x.enter_context(nc.sbuf_tensor("xnbuf", [128, 2 * D], BF16))
    stat6 = p1x.enter_context(nc.sbuf_tensor("stat6", [128, 2 * 6 * NST], F32))
    mv2 = p1x.enter_context(nc.sbuf_tensor("mv2", [128, 4], F32))
    sd1 = p1x.enter_context(nc.sbuf_tensor("sd1", [128, 2], F32))
    rsb = p1x.enter_context(nc.sbuf_tensor("rsb", [128, 2], F32))
    nbb = p1x.enter_context(nc.sbuf_tensor("nbb", [128, 2], F32))

    p1_ve = [0] * TT
    p1_act = [0] * TT

    def p1_load(i):
        if i >= TT:
            return
        b = i % 2
        if i >= 2:
            sc.wait(Sve, p1_ve[i - 2])
        sc.op("dma_start", out=xbuf[:, b * D:(b + 1) * D],
              in_=x[i * 128:(i + 1) * 128, :]).inc(Dx[b])

    p1_load(0)
    p1_load(1)
    for i in range(TT):
        b = i % 2
        xs = xbuf[:, b * D:(b + 1) * D]
        ve.wait(Dx[b])
        if i >= 2:
            ve.wait(Ssc, p1_act[i - 2])
        for j in range(NST):
            w = min(512, D - j * 512)
            ve.op("bn_stats", out=stat6[:, (b * NST + j) * 6:(b * NST + j + 1) * 6],
                  in_=xs[:, j * 512:j * 512 + w]).inc(Sve)
        ve.wait(Sve)
        ve.op("bn_aggr", out=mv2[:, b * 2:b * 2 + 2],
              in_=stat6[:, b * NST * 6:(b + 1) * NST * 6]).inc(Sve)
        sc.wait(Sve)
        sc.wait(Dcn)
        sc.op("activation", out=sd1[:, b:b + 1], in_=mv2[:, b * 2 + 1:b * 2 + 2],
              func=AF.Sqrt, bias=fcb[:, 0:1]).inc(Ssc)
        ve.wait(Ssc)
        ve.op("reciprocal", out=rsb[:, b:b + 1], in_=sd1[:, b:b + 1]).inc(Sve)
        ve.wait(Sve)
        ve.op("scalar_tensor_tensor", out=nbb[:, b:b + 1], in0=mv2[:, b * 2:b * 2 + 1],
              scalar=-1.0, in1=rsb[:, b:b + 1], op0=ALU.mult, op1=ALU.mult).inc(Sve)
        p1_ve[i] = Sve.n
        sc.wait(Sve, p1_ve[i])
        sc.wait(Dx[b])
        sc.wait(Dxw[b])
        sc.op("activation", out=xnbuf[:, b * D:(b + 1) * D], in_=xs,
              func=AF.Identity, scale=rsb[:, b:b + 1], bias=nbb[:, b:b + 1]).inc(Ssc)
        p1_act[i] = Ssc.n
        sc.wait(Ssc, p1_act[i])
        sc.op("dma_start", out=s_xn[i * 128:(i + 1) * 128, :],
              in_=xnbuf[:, b * D:(b + 1) * D]).inc(Dxw[b])
        p1_load(i + 2)
    p1_sve, p1_ssc = Sve.n, Ssc.n
    p1x.close()

    p1_release = [(Sve, p1_sve), (Ssc, p1_ssc),
                  (Dx[0], Dx[0].n), (Dx[1], Dx[1].n),
                  (Dxw[0], Dxw[0].n), (Dxw[1], Dxw[1].n)]

    st1_te = [0] * HT
    st1_ev = [0] * HT
    st2_te = [0] * HT
    s1 = {"n": 0}
    s2 = {"n": 0}
    w2h_mark = []

    def pump_s1(k):
        for _ in range(k):
            a = s1["n"]
            if a >= HT:
                return
            s1["n"] += 1
            slot = a % NFS
            sy.wait(Dc2)
            if a >= NFS:
                sy.wait(Ste, st1_te[a - NFS])
            sy.op("dma_start", out=w2tb[:, slot * D:(slot + 1) * D],
                  in_=s_w2[:, a * 128:(a + 1) * 128], transpose=True).inc(Dw2t[slot])
            te.wait(Dw2t[slot])
            te.wait(Dcn)
            b = take()
            for (o, w) in _ch(D):
                st1_te[a] = te.op("matmul", PS[b][:, o:o + w], cH128[:],
                                  w2tb[:, slot * D + o:slot * D + o + w],
                                  start=True, stop=True).inc(Ste)
            ve.wait(Ste, st1_te[a])
            ve.wait(Dt1w[slot])
            sve_v = ve.op("tensor_copy", out=foldev[:, slot * D:(slot + 1) * D],
                          in_=PS[b][:, 0:D]).inc(Sve)
            st1_ev[a] = sve_v
            ps_rel[b] = [(Sve, sve_v)]
            sc.wait(Sve, sve_v)
            sc.op("dma_start", out=t1_swap[a],
                  in_=foldev[:, slot * D:(slot + 1) * D]).inc(Dt1w[slot])

    def pump_s2(k):
        for _ in range(k):
            t = s2["n"]
            if t >= HT:
                return
            s2["n"] += 1
            slot = t % NFS
            if t == 0:
                assert s1["n"] == HT
                for s_ in Dt1w:
                    gp.wait(s_)
                    ve.wait(s_)
                gp.wait(Ste, max(st1_te))
                ve.wait(Sve, max(st1_ev))
            if t >= NFS:
                gp.wait(Ste, st2_te[t - NFS])
            gp.op("dma_start", out=w2tb[:, slot * D:(slot + 1) * D],
                  in_=s_t1[t * 128:(t + 1) * 128, :]).inc(Dw2u[slot])
            te.wait(Dw2u[slot])
            b = take()
            for (o, w) in _ch(D):
                st2_te[t] = te.op("matmul", PS[b][:, o:o + w], cBD[:],
                                  w2tb[:, slot * D + o:slot * D + o + w],
                                  start=True, stop=True).inc(Ste)
            ve.wait(Ste, st2_te[t])
            ve.wait(Dw2hw[slot])
            ve.wait(Dcn)
            sve_v = ve.op("tensor_scalar", out=foldev[:, slot * D:(slot + 1) * D],
                          in0=PS[b][:, 0:D], scalar1=fcb[:, 1:2], scalar2=None,
                          op0=ALU.mult).inc(Sve)
            ps_rel[b] = [(Sve, sve_v)]
            gp.wait(Sve, sve_v)
            gp.op("dma_start", out=w2h_unswap[t],
                  in_=foldev[:, slot * D:(slot + 1) * D]).inc(Dw2hw[slot])
            if s2["n"] % GCT == 0:
                w2h_mark.append([(s_, s_.n) for s_ in Dw2hw])

    sy.wait(Dco)
    for d1 in range(DT):
        sy.op("dma_start", out=wotb[:, d1 * D:(d1 + 1) * D],
              in_=s_wo[:, d1 * 128:(d1 + 1) * 128], transpose=True).inc(Dwot)

    te.wait(Dwot)
    te.wait(Dwv)
    wvo_te = [0] * DT
    for d2t in range(DT):
        b = take()
        for d1 in range(DT):
            lhs = wotb[:, d1 * D + d2t * 128: d1 * D + (d2t + 1) * 128]
            for (o, w) in _ch(D):
                hh = te.op("matmul", PS[b][:, o:o + w], lhs,
                           wvb[:, d1 * D + o:d1 * D + o + w],
                           start=(d1 == 0), stop=(d1 == DT - 1))
                if d1 == DT - 1:
                    wvo_te[d2t] = hh.inc(Ste)
        ve.wait(Ste, wvo_te[d2t])
        if d2t >= 1:
            ve.wait(Dvw[0])
        sve_v = ve.op("tensor_copy", out=wvoev[:, 0:D],
                      in_=PS[b][:, 0:D]).inc(Sve)
        ps_rel[b] = [(Sve, sve_v)]
        gp.wait(Sve, sve_v)
        gp.op("dma_start", out=s_wvoT[d2t * 128:(d2t + 1) * 128, :],
              in_=wvoev[:, 0:D]).inc(Dvw[0])
    p3_te_end = Ste.n
    p3_ve_end = Sve.n

    p2v_release = [(Ste, p3_te_end), (Sve, p3_ve_end),
                   (Dvw[0], Dvw[0].n), (Dvw[1], Dvw[1].n), (Dwv, Dwv.n)]
    p2v.close()

    p46 = contextlib.ExitStack()
    xnT = p46.enter_context(nc.sbuf_tensor("xnT", [128, DT * T], BF16, is_transpose_dest=True))
    wqkT = p46.enter_context(nc.sbuf_tensor("wqkT", [128, 2 * DT * R], BF16, is_transpose_dest=True))
    qfT = p46.enter_context(nc.sbuf_tensor("qfT", [R, T], BF16))
    kfT = p46.enter_context(nc.sbuf_tensor("kfT", [R, T], BF16))

    boundary(p2v_release + p1_release)

    sy.wait(Dxw[0])
    sy.wait(Dxw[1])
    for d in range(DT):
        sy.op("dma_start", out=xnT[:, d * T:(d + 1) * T],
              in_=s_xn[:, d * 128:(d + 1) * 128], transpose=True).inc(DxnT)
    sy.wait(Dcq)
    sy.wait(Dck)
    for d in range(DT):
        sy.op("dma_start", out=wqkT[:, d * R:(d + 1) * R],
              in_=s_wq[:, d * 128:(d + 1) * 128], transpose=True).inc(Dwqk)
        sy.op("dma_start", out=wqkT[:, (DT + d) * R:(DT + d + 1) * R],
              in_=s_wk[:, d * 128:(d + 1) * 128], transpose=True).inc(Dwqk)
    wvob = p46.enter_context(nc.sbuf_tensor("wvob", [128, DT * D], BF16, is_transpose_dest=True))
    sy.wait(Dvw[0])
    sy.wait(Dvw[1])
    for d in range(DT):
        sy.op("dma_start", out=wvob[:, d * D:(d + 1) * D],
              in_=s_wvoT[:, d * 128:(d + 1) * 128], transpose=True).inc(Dwvo)

    te.wait(DxnT)
    te.wait(Dwqk)
    for (o, w) in _ch(T):
        b0 = take()
        b1 = take()
        for d in range(DT):
            h0 = te.op("matmul", PS[b0][0:R, o:o + w], wqkT[:, d * R:(d + 1) * R],
                       xnT[:, d * T + o:d * T + o + w], start=(d == 0), stop=(d == DT - 1))
        h0.inc(Ste)
        for d in range(DT):
            h1 = te.op("matmul", PS[b1][0:R, o:o + w], wqkT[:, (DT + d) * R:(DT + d + 1) * R],
                       xnT[:, d * T + o:d * T + o + w], start=(d == 0), stop=(d == DT - 1))
        qk_te = h1.inc(Ste)
        sc.wait(Ste, qk_te)
        sc.op("activation", out=qfT[:, o:o + w], in_=PS[b0][0:R, o:o + w], func=AF.Copy).inc(Ssc)
        qk_sc = sc.op("activation", out=kfT[:, o:o + w], in_=PS[b1][0:R, o:o + w],
                      func=AF.Copy).inc(Ssc)
        ps_rel[b0] = [(Ssc, qk_sc)]
        ps_rel[b1] = [(Ssc, qk_sc)]
    qk_done_sc = Ssc.n

    abuf = p46.enter_context(nc.sbuf_tensor("abuf", [128, 2 * T], BF16))
    negm = p46.enter_context(nc.sbuf_tensor("negm", [128, 2], F32))
    sumb = p46.enter_context(nc.sbuf_tensor("sumb", [128, 2], F32))
    uev = p46.enter_context(nc.sbuf_tensor("uev", [128, 2 * D], BF16))

    te.wait(Ssc, qk_done_sc)
    a_sc = [0] * TT
    for i in range(TT):
        bs = i % 2
        b = take()
        s_te = 0
        for (o, w) in _ch(T):
            s_te = te.op("matmul", PS[b][:, o:o + w], qfT[:, i * 128:(i + 1) * 128],
                         kfT[:, o:o + w], start=True, stop=True).inc(Ste)
        ve.wait(Ste, s_te)
        if i >= 2:
            ve.wait(Ssc, a_sc[i - 2])
        negm_v = ve.op("reduce_max", out=negm[:, bs:bs + 1], in_=PS[b][:, 0:T], axis=AXX,
                       negate=True).inc(Sve)
        sc.wait(Sve, negm_v)
        sc.wait(Daw[bs])
        a_sc[i] = sc.op("activation", out=abuf[:, bs * T:(bs + 1) * T], in_=PS[b][:, 0:T],
                        func=AF.Exp, bias=negm[:, bs:bs + 1]).inc(Ssc)
        ps_rel[b] = [(Ssc, a_sc[i]), (Sve, negm_v)]
        ve.wait(Ssc, a_sc[i])
        ve.op("reduce_sum", out=sumb[:, bs:bs + 1], in_=abuf[:, bs * T:(bs + 1) * T],
              axis=AXX).inc(Sve)
        ve.wait(Sve)
        ve.op("reciprocal", out=recipb[:, i:i + 1], in_=sumb[:, bs:bs + 1]).inc(Sve)
        gp.wait(Ssc, a_sc[i])
        gp.op("dma_start", out=s_a[i * 128:(i + 1) * 128, :],
              in_=abuf[:, bs * T:(bs + 1) * T]).inc(Daw[bs])
        pump_s1(max(2, (HT + TT - 1) // TT))
    pump_s1(HT)
    recip_done = Sve.n

    te.wait(Dwvo)
    for i in range(TT):
        bs = i % 2
        b = take()
        u_te = 0
        for d in range(DT):
            lhs = xnT[:, d * T + i * 128: d * T + (i + 1) * 128]
            for (o, w) in _ch(D):
                hh = te.op("matmul", PS[b][:, o:o + w], lhs,
                           wvob[:, d * D + o:d * D + o + w],
                           start=(d == 0), stop=(d == DT - 1))
                if d == DT - 1:
                    u_te = hh.inc(Ste)
        sc.wait(Ste, u_te)
        sc.wait(Duw[bs])
        ssc_v = sc.op("activation", out=uev[:, bs * D:(bs + 1) * D], in_=PS[b][:, 0:D],
                      func=AF.Copy).inc(Ssc)
        ps_rel[b] = [(Ssc, ssc_v)]
        gp.wait(Ssc, ssc_v)
        gp.op("dma_start", out=s_u[i * 128:(i + 1) * 128, :],
              in_=uev[:, bs * D:(bs + 1) * D]).inc(Duw[bs])
        pump_s2(3)
    recip_done = recip_done

    HH = H // 2
    pret_half_mark = [None, None]
    pret_state = {"it": 0}

    def pump_pret(k):
        for _ in range(k):
            it = pret_state["it"]
            if it >= 4 * DT:
                return
            pret_state["it"] += 1
            hf = it // (2 * DT)
            rem = it % (2 * DT)
            d = rem // 2
            (s_src, d_wr, dcast) = [(s_w1, w1q_wr, Dc1h[hf]), (s_w3, w3q_wr, Dc3h[hf])][rem % 2]
            slot = 2 * hf + (it % 2)
            sy.wait(dcast)
            if hf == 1:
                sy.wait(Dptw[slot - 2])
            sy.wait(Dptw[slot])
            sy.op("dma_start", out=ptb[:, (slot % 2) * HH:(slot % 2 + 1) * HH],
                  in_=s_src[hf * HH:(hf + 1) * HH, d * 128:(d + 1) * 128],
                  transpose=True).inc(Dpt[slot])
            gp.wait(Dpt[slot])
            gp.op("dma_start", out=d_wr[:, d, hf * shf:(hf + 1) * shf, :],
                  in_=ptb[:, (slot % 2) * HH:(slot % 2 + 1) * HH]).inc(Dptw[slot])
            if pret_state["it"] == 2 * DT:
                pret_half_mark[0] = (Dptw[0].n, Dptw[1].n)
            if pret_state["it"] == 4 * DT:
                pret_half_mark[1] = (Dptw[2].n, Dptw[3].n)

    for _ in range(2 * DT):
        pump_pret(1)
        pump_s2(1)
    pump_s2(HT)
    assert s2["n"] == HT and len(w2h_mark) == NGRP

    p46_te_end = Ste.n
    p46.close()
    pfold_release = [(Ste, max(max(st2_te), 1)),
                     *[(s_, s_.n) for s_ in Dw2u], *[(s_, s_.n) for s_ in Dw2hw]]
    pfold.close()

    p7 = contextlib.ExitStack()
    xnTh = p7.enter_context(nc.sbuf_tensor("xnTh", [128, DT * TH], BF16, is_transpose_dest=True))
    acch = p7.enter_context(nc.sbuf_tensor("acch", [128, THT * D], BF16))
    gbuf = p7.enter_context(nc.sbuf_tensor("gbuf", [128, GCT * TH], BF16))
    w1b = p7.enter_context(nc.sbuf_tensor("w1b", [128, 2 * DT * SCW], BF16))
    w3b = p7.enter_context(nc.sbuf_tensor("w3b", [128, 2 * DT * SCW], BF16))
    w2hb = p7.enter_context(nc.sbuf_tensor("w2hb", [128, GCT * D], BF16))
    silt = p7.enter_context(nc.sbuf_tensor("silt", [128, 2 * TH], BF16))

    boundary([(Ste, p46_te_end), (Sve, Sve.n), (Ssc, Ssc.n),
              (Daw[0], Daw[0].n), (Daw[1], Daw[1].n),
              (Duw[0], Duw[0].n), (Duw[1], Duw[1].n),
              *pfold_release])

    sil_guard = [0, 0]
    acc_spill = [0, 0]
    for half in range(2):
        r0h = half * TH
        for d in range(DT):
            sy.op("dma_start", out=xnTh[:, d * TH:(d + 1) * TH],
                  in_=s_xn[r0h:r0h + TH, d * 128:(d + 1) * 128], transpose=True).inc(DxnTh)
        xnTh_done = DxnTh.n

        sup_te = [0] * NSUP

        def p7_load_super(s, _sup_te=sup_te):
            if s >= NSUP:
                return
            pump_pret(4 * DT)if pret_half_mark[s // shf] is None else None
            bsl = s % 2
            hf = s // shf
            m0, m1 = pret_half_mark[hf]
            sy.wait(Dptw[2 * hf], m0)
            sy.wait(Dptw[2 * hf + 1], m1)
            gp.wait(Dptw[2 * hf], m0)
            gp.wait(Dptw[2 * hf + 1], m1)
            if s >= 2:
                sy.wait(Ste, _sup_te[s - 2])
                gp.wait(Ste, _sup_te[s - 2])
            sy.op("dma_start", out=w1b[:, bsl * DT * SCW:(bsl + 1) * DT * SCW],
                  in_=w1q_sup[s]).inc(Dw1[bsl])
            gp.op("dma_start", out=w3b[:, bsl * DT * SCW:(bsl + 1) * DT * SCW],
                  in_=w3q_sup[s]).inc(Dw3[bsl])

        p7_load_super(0)
        p7_load_super(1)

        te.wait(DxnTh, xnTh_done)
        acc_ve = [0] * THT
        for g in range(NGRP):
            gp.wait(Ste)
            for s_, v in w2h_mark[g]:
                gp.wait(s_, v)
            gp.op("dma_start", out=w2hb[:, 0:GCT * D],
                  in_=w2h_rows[:, g * GCT:(g + 1) * GCT, :]).inc(Dw2hb)

            g_mul = 0
            for si in range(SUPG):
                s = g * SUPG + si
                bsl = s % 2
                te.wait(Dw1[bsl])
                te.wait(Dw3[bsl])
                for ct in range(SCT):
                    c_glob = s * SCT + ct
                    cg = c_glob - g * GCT
                    slot = c_glob % 2
                    pb = take()
                    gate_te = lin_te = 0
                    for d in range(DT):
                        lhs = w1b[:, (bsl * DT + d) * SCW + ct * 128:(bsl * DT + d) * SCW + (ct + 1) * 128]
                        for (o, w) in _ch(TH):
                            hh = te.op("matmul", PS[pb][:, o:o + w], lhs,
                                       xnTh[:, d * TH + o:d * TH + o + w],
                                       start=(d == 0), stop=(d == DT - 1))
                            if d == DT - 1:
                                gate_te = hh.inc(Ste)
                    for d in range(DT):
                        lhs = w3b[:, (bsl * DT + d) * SCW + ct * 128:(bsl * DT + d) * SCW + (ct + 1) * 128]
                        for (o, w) in _ch(TH):
                            hh = te.op("matmul", PS[pb][:, 1024 + o:1024 + o + w], lhs,
                                       xnTh[:, d * TH + o:d * TH + o + w],
                                       start=(d == 0), stop=(d == DT - 1))
                            if d == DT - 1:
                                lin_te = hh.inc(Ste)
                    sc.wait(Ste, gate_te)
                    sc.wait(Sve, sil_guard[slot])
                    sil_sc = sc.op("activation", out=silt[:, slot * TH:(slot + 1) * TH],
                                   in_=PS[pb][:, 0:TH], func=AF.Silu).inc(Ssc)
                    ve.wait(Ssc, sil_sc)
                    ve.wait(Ste, lin_te)
                    g_mul = ve.op("tensor_tensor", out=gbuf[:, cg * TH:(cg + 1) * TH],
                                  in0=silt[:, slot * TH:(slot + 1) * TH],
                                  in1=PS[pb][:, 1024:1024 + TH], op=ALU.mult).inc(Sve)
                    sil_guard[slot] = g_mul
                    ps_rel[pb] = [(Sve, g_mul), (Ssc, sil_sc)]
                sup_te[s] = Ste.n
                if half == 0:
                    pump_pret(2 + 4 * DT // max(1, NSUP))
                p7_load_super(s + 2)
            te.wait(Sve, g_mul)
            te.wait(Dw2hb)
            for j in range(THT):
                pb = take()
                dn_te = 0
                for ci in range(GCT):
                    lhs = gbuf[:, ci * TH + j * 128: ci * TH + (j + 1) * 128]
                    for (o, w) in _ch(D):
                        hh = te.op("matmul", PS[pb][:, o:o + w], lhs,
                                   w2hb[:, ci * D + o:ci * D + o + w],
                                   start=(ci == 0), stop=(ci == GCT - 1))
                        if ci == GCT - 1:
                            dn_te = hh.inc(Ste)
                ve.wait(Ste, dn_te)
                if g == 0:
                    if half == 1 and j == 0:
                        ve.wait(Daccw, acc_spill[0])
                    hh = ve.op("tensor_copy", out=acch[:, j * D:(j + 1) * D], in_=PS[pb][:, 0:D])
                else:
                    ve.wait(Sve, acc_ve[j])
                    hh = ve.op("tensor_tensor", out=acch[:, j * D:(j + 1) * D],
                               in0=PS[pb][:, 0:D], in1=acch[:, j * D:(j + 1) * D], op=ALU.add)
                acc_ve[j] = hh.inc(Sve)
                ps_rel[pb] = [(Sve, acc_ve[j])]
            if g + 1 < NGRP:
                ve.wait(Ste, Ste.n)
        for j in range(THT):
            gp.wait(Sve, acc_ve[j])
        gp.op("dma_start", out=acc_rows[:, half * THT:(half + 1) * THT, :],
              in_=acch[:, 0:THT * D]).inc(Daccw)
        acc_spill[half] = Daccw.n
        if half == 0:
            sy.wait(Ste, Ste.n)
            ve.wait(Ste, Ste.n)
    p7_te_end = Ste.n
    p7.close()
    p4t_release = [*[(s_, s_.n) for s_ in Dpt], *[(s_, s_.n) for s_ in Dptw]]
    p4t.close()

    p8 = contextlib.ExitStack()
    ubuf = p8.enter_context(nc.sbuf_tensor("ubuf", [128, DT * D], BF16))
    NQC = max(1, T // 512)
    QW = T // NQC
    QT = QW // 128
    atb = p8.enter_context(nc.sbuf_tensor("atb", [128, 2 * TT * QW], BF16, is_transpose_dest=True))
    xrb = p8.enter_context(nc.sbuf_tensor("xrb", [128, 2 * D], F32))
    accrb = p8.enter_context(nc.sbuf_tensor("accrb", [128, 2 * D], BF16))
    t2b = p8.enter_context(nc.sbuf_tensor("t2b", [128, 2 * D], F32))
    outb = p8.enter_context(nc.sbuf_tensor("outb", [128, 2 * D], F32))

    boundary([(Ste, p7_te_end), (Sve, Sve.n), (Ssc, Ssc.n), (Daccw, Daccw.n),
              *p4t_release])
    gp.wait(Duw[0])
    gp.wait(Duw[1])
    for tk in range(TT):
        gp.op("dma_start", out=ubuf[:, tk * D:(tk + 1) * D],
              in_=s_u[tk * 128:(tk + 1) * 128, :]).inc(Dub)

    at_te = [0] * NQC
    sy.wait(Daw[0])
    sy.wait(Daw[1])
    ve.wait(Sve, recip_done)

    def p8_load_at(q):
        if q >= NQC:
            return
        bq = q % 2
        if q >= 2:
            sy.wait(Ste, at_te[q - 2])
        for tk in range(TT):
            sy.op("dma_start", out=atb[:, (bq * TT + tk) * QW:(bq * TT + tk + 1) * QW],
                  in_=s_a[q * QW:(q + 1) * QW, tk * 128:(tk + 1) * 128],
                  transpose=True).inc(Dat[bq])

    p8_load_at(0)
    p8_load_at(1)
    te.wait(Dub)
    out_ve = [0] * TT

    def p8_load_xacc(i):
        if i >= TT:
            return
        bs = i % 2
        if i >= 2:
            sc.wait(Sve, out_ve[i - 2])
        sc.op("dma_start", out=xrb[:, bs * D:(bs + 1) * D],
              in_=x[i * 128:(i + 1) * 128, :]).inc(Dxa[bs])
        if i == 0:
            sc.wait(Daccw)
        sc.op("dma_start", out=accrb[:, bs * D:(bs + 1) * D],
              in_=s_acc[i * 128:(i + 1) * 128, :]).inc(Dxa[bs])

    p8_load_xacc(0)
    p8_load_xacc(1)
    for q in range(NQC):
        bq = q % 2
        te.wait(Dat[bq])
        for j in range(QT):
            i = q * QT + j
            bs = i % 2
            b = take()
            a_te = 0
            for tk in range(TT):
                lhs = atb[:, (bq * TT + tk) * QW + j * 128:(bq * TT + tk) * QW + (j + 1) * 128]
                for (o, w) in _ch(D):
                    hh = te.op("matmul", PS[b][:, o:o + w], lhs,
                               ubuf[:, tk * D + o:tk * D + o + w],
                               start=(tk == 0), stop=(tk == TT - 1))
                    if tk == TT - 1:
                        a_te = hh.inc(Ste)
            if j == QT - 1:
                at_te[q] = a_te
            ve.wait(Ste, a_te)
            ve.wait(Dxa[bs])
            ve.wait(Dow[bs])
            ve.op("tensor_tensor", out=t2b[:, bs * D:(bs + 1) * D],
                  in0=xrb[:, bs * D:(bs + 1) * D], in1=accrb[:, bs * D:(bs + 1) * D],
                  op=ALU.add).inc(Sve)
            ve.wait(Sve)
            out_ve[i] = ve.op("scalar_tensor_tensor", out=outb[:, bs * D:(bs + 1) * D],
                              in0=PS[b][:, 0:D], scalar=recipb[:, i:i + 1],
                              in1=t2b[:, bs * D:(bs + 1) * D],
                              op0=ALU.mult, op1=ALU.add).inc(Sve)
            ps_rel[b] = [(Sve, out_ve[i])]
            gp.wait(Sve, out_ve[i])
            gp.op("dma_start", out=out[i * 128:(i + 1) * 128, :],
                  in_=outb[:, bs * D:(bs + 1) * D]).inc(Dow[bs])
            p8_load_xacc(i + 2)
        p8_load_at(q + 2)
    gp.wait(Dow[0])
    gp.wait(Dow[1])
    p8.close()

    with nc.Block() as block:
        @block.sync
        def _(eng):
            sy.play(eng)

        @block.scalar
        def _(eng):
            sc.play(eng)

        @block.vector
        def _(eng):
            ve.play(eng)

        @block.tensor
        def _(eng):
            te.play(eng)

        @block.gpsimd
        def _(eng):
            gp.play(eng)

    stack.close()
    return nc


def kernel(L0_attn_hidden_states, L0_attn_Wq, L0_attn_Wk, L0_attn_Wv, L0_attn_Wo,
           L0_ffn_W1, L0_ffn_W2, L0_ffn_W3):
    x = np.ascontiguousarray(np.asarray(L0_attn_hidden_states, dtype=np.float32))
    B, T, D = x.shape
    H = L0_ffn_W1.shape[0]
    R = L0_attn_Wq.shape[0]
    nc = build_program(T, D, H, R)
    h128, bd = make_consts(H)
    common = {
        "wq": np.ascontiguousarray(np.asarray(L0_attn_Wq, np.float32)),
        "wk": np.ascontiguousarray(np.asarray(L0_attn_Wk, np.float32)),
        "wv": np.ascontiguousarray(np.asarray(L0_attn_Wv, np.float32)),
        "wo": np.ascontiguousarray(np.asarray(L0_attn_Wo, np.float32)),
        "w1": np.ascontiguousarray(np.asarray(L0_ffn_W1, np.float32)),
        "w2": np.ascontiguousarray(np.asarray(L0_ffn_W2, np.float32)),
        "w3": np.ascontiguousarray(np.asarray(L0_ffn_W3, np.float32)),
        "h128": h128,
        "bd": bd,
        "fconst": np.tile(np.array([[EPS, 1.0 / np.sqrt(H)]], np.float32), (128, 1)),
    }
    in_maps = [{"x": np.ascontiguousarray(x[i]), **common} for i in range(B)]
    res = run_bass_kernel_spmd(nc, in_maps, core_ids=list(range(B)))
    return np.stack([np.asarray(r["out"], dtype=np.float32) for r in res.results], axis=0)


# version 11
# speedup vs baseline: 1.0386x; 1.0372x over previous
import contextlib
import numpy as np
import ml_dtypes
import concourse.bass as bass
import concourse.mybir as mybir
from concourse.bass_utils import run_bass_kernel_spmd

F32 = mybir.dt.float32
BF16 = mybir.dt.bfloat16
AXX = mybir.AxisListType.X
AF = mybir.ActivationFunctionType
ALU = mybir.AluOpType
BF = ml_dtypes.bfloat16
EPS = 1e-5


def _hadamard(n):
    h = np.array([[1.0]], dtype=np.float64)
    while h.shape[0] < n:
        h = np.block([[h, h], [h, -h]])
    return h


def make_consts(H):
    NHI = H // 128
    E = 128 // NHI
    h128 = _hadamard(128).astype(BF)
    bd = np.kron(np.eye(E), _hadamard(NHI)).astype(BF)
    return h128, bd


class _Sem:
    def __init__(self, h, name):
        self.h = h
        self.name = name
        self.n = 0


class _Rec:
    def __init__(self, name):
        self.name = name
        self.ops = []
        self._hwm = {}

    def wait(self, sem, val=None):
        v = sem.n if val is None else val
        if v <= 0 or self._hwm.get(sem.name, -1) >= v:
            return
        self._hwm[sem.name] = v
        self.ops.append(("wait_ge", (sem, v), {}, []))

    def op(self, meth, *a, **k):
        entry = [meth, a, k, []]
        self.ops.append(entry)

        class _H:
            def inc(_s, sem, amt=None):
                d = 16 if meth == "dma_start" else 1
                if amt is not None:
                    d = amt
                sem.n += d
                entry[3].append((sem, d))
                return sem.n
        return _H()

    def play(self, eng):
        for meth, a, k, incs in self.ops:
            if meth == "wait_ge":
                eng.wait_ge(a[0].h, a[1])
                continue
            ins = getattr(eng, meth)(*a, **k)
            for sem, d in incs:
                ins = ins.then_inc(sem.h, d)


def _ch(total, step=512):
    return [(o, min(step, total - o)) for o in range(0, total, step)]


def build_program(T, D, H, R=64):
    DT, TT, HT = D // 128, T // 128, H // 128
    NHI = HT
    E = 128 // NHI
    assert NHI * 128 == H and E * NHI == 128
    TH = T // 2
    THT = TH // 128
    SCW = min(512, H)
    SCT = SCW // 128
    NSUP = H // SCW
    GCT = min(8, HT)
    NGRP = HT // GCT
    SUPG = GCT // SCT
    shf = NSUP // 2
    NFS = 3

    nc = bass.Bass("TRN2", target_bir_lowering=False)

    x = nc.dram_tensor("x", [T, D], F32, kind="ExternalInput").ap()
    wq = nc.dram_tensor("wq", [R, D], F32, kind="ExternalInput").ap()
    wk = nc.dram_tensor("wk", [R, D], F32, kind="ExternalInput").ap()
    wv = nc.dram_tensor("wv", [D, D], F32, kind="ExternalInput").ap()
    wo = nc.dram_tensor("wo", [D, D], F32, kind="ExternalInput").ap()
    w1 = nc.dram_tensor("w1", [H, D], F32, kind="ExternalInput").ap()
    w2 = nc.dram_tensor("w2", [D, H], F32, kind="ExternalInput").ap()
    w3 = nc.dram_tensor("w3", [H, D], F32, kind="ExternalInput").ap()
    h128c = nc.dram_tensor("h128", [128, 128], BF16, kind="ExternalInput").ap()
    bdc = nc.dram_tensor("bd", [128, 128], BF16, kind="ExternalInput").ap()
    fconst = nc.dram_tensor("fconst", [128, 2], F32, kind="ExternalInput").ap()
    out = nc.dram_tensor("out", [T, D], F32, kind="ExternalOutput").ap()

    s_wq = nc.dram_tensor("s_wq", [R, D], BF16, kind="Internal").ap()
    s_wk = nc.dram_tensor("s_wk", [R, D], BF16, kind="Internal").ap()
    s_wo = nc.dram_tensor("s_wo", [D, D], BF16, kind="Internal").ap()
    s_w1 = nc.dram_tensor("s_w1", [H, D], BF16, kind="Internal").ap()
    s_w2 = nc.dram_tensor("s_w2", [D, H], BF16, kind="Internal").ap()
    s_w3 = nc.dram_tensor("s_w3", [H, D], BF16, kind="Internal").ap()
    s_xn = nc.dram_tensor("s_xn", [T, D], BF16, kind="Internal").ap()
    s_t1 = nc.dram_tensor("s_t1", [H, D], BF16, kind="Internal").ap()
    s_w2h = nc.dram_tensor("s_w2h", [H, D], BF16, kind="Internal").ap()
    s_wvoT = nc.dram_tensor("s_wvoT", [D, D], BF16, kind="Internal").ap()
    s_u = nc.dram_tensor("s_u", [T, D], BF16, kind="Internal").ap()
    s_a = nc.dram_tensor("s_a", [T, T], BF16, kind="Internal").ap()
    s_acc = nc.dram_tensor("s_acc", [T, D], BF16, kind="Internal").ap()
    s_w1q = nc.dram_tensor("s_w1q", [NSUP * 128 * DT * SCW], BF16, kind="Internal").ap()
    s_w3q = nc.dram_tensor("s_w3q", [NSUP * 128 * DT * SCW], BF16, kind="Internal").ap()

    t1_swap = s_t1.rearrange("(b a) d -> a b d", a=NHI)
    w2h_unswap = s_w2h.rearrange("(m tau e) d -> tau e m d", m=NHI, e=E)
    w2h_rows = s_w2h.rearrange("(n p) d -> p n d", p=128)
    acc_rows = s_acc.rearrange("(n p) d -> p n d", p=128)
    w1q_sup = s_w1q.rearrange("(s p dt c) -> s p (dt c)", s=NSUP, p=128, dt=DT)
    w3q_sup = s_w3q.rearrange("(s p dt c) -> s p (dt c)", s=NSUP, p=128, dt=DT)
    w1q_wr = s_w1q.rearrange("(s p dt c) -> p dt s c", s=NSUP, p=128, dt=DT)
    w3q_wr = s_w3q.rearrange("(s p dt c) -> p dt s c", s=NSUP, p=128, dt=DT)

    sy, sc, ve, te, gp = (_Rec(n) for n in ("sy", "sc", "ve", "te", "gp"))
    stack = contextlib.ExitStack()

    def sem(name):
        return _Sem(stack.enter_context(nc.semaphore(name)), name)

    Ste, Ssc, Sve = sem("Ste"), sem("Ssc"), sem("Sve")
    Dcn = sem("Dcn")
    Dcq, Dck, Dco, Dc2 = sem("Dcq"), sem("Dck"), sem("Dco"), sem("Dc2")
    Dc1h = [sem("Dc1a"), sem("Dc1b")]
    Dc3h = [sem("Dc3a"), sem("Dc3b")]
    Dx = [sem("Dx0"), sem("Dx1")]
    Dxw = [sem("Dxw0"), sem("Dxw1")]
    Dw2t = [sem(f"Dw2t{i}") for i in range(NFS)]
    Dt1w = [sem(f"Dt1w{i}") for i in range(NFS)]
    Dw2u = [sem(f"Dw2u{i}") for i in range(NFS)]
    Dw2hw = [sem(f"Dw2hw{i}") for i in range(NFS)]
    Dwot = sem("Dwot")
    Dwv = sem("Dwv")
    Dvw = [sem("Dvw0"), sem("Dvw1")]
    DxnT = sem("DxnT")
    Dwqk = sem("Dwqk")
    Daw = [sem("Daw0"), sem("Daw1")]
    Dwvo = sem("Dwvo")
    Duw = [sem("Duw0"), sem("Duw1")]
    DxnTh = sem("DxnTh")
    Dw1 = [sem("Dw10"), sem("Dw11")]
    Dw3 = [sem("Dw30"), sem("Dw31")]
    Dw2hb = sem("Dw2hb")
    Daccw = sem("Daccw")
    Dub = sem("Dub")
    Dat = [sem("Dat0"), sem("Dat1")]
    Dxa = [sem("Dxa0"), sem("Dxa1")]
    Dow = [sem("Dow0"), sem("Dow1")]
    Dpt = [sem(f"Dpt{i}") for i in range(4)]
    Dptw = [sem(f"Dptw{i}") for i in range(4)]

    cH128 = stack.enter_context(nc.sbuf_tensor("cH128", [128, 128], BF16))
    cBD = stack.enter_context(nc.sbuf_tensor("cBD", [128, 128], BF16))
    recipb = stack.enter_context(nc.sbuf_tensor("recipb", [128, TT], F32))
    fcb = stack.enter_context(nc.sbuf_tensor("fcb", [128, 2], F32))

    ps0 = stack.enter_context(nc.psum_tensor("ps0", [128, 2048], F32))
    ps1 = stack.enter_context(nc.psum_tensor("ps1", [128, 2048], F32))
    PS = [ps0, ps1]
    ps_rel = [[], []]
    srr = [0]

    def take():
        b = srr[0] % 2
        srr[0] += 1
        for s_, v in ps_rel[b]:
            te.wait(s_, v)
        ps_rel[b] = []
        return b

    def boundary(release, recs=None):
        for rec in (recs or (sy, sc, ve, gp)):
            for s_, v in release:
                rec.wait(s_, v)

    p4t = contextlib.ExitStack()
    ptb = p4t.enter_context(nc.sbuf_tensor("ptb", [128, H], BF16, is_transpose_dest=True))
    pfold = contextlib.ExitStack()
    w2tb = pfold.enter_context(nc.sbuf_tensor("w2tb", [128, NFS * D], BF16, is_transpose_dest=True))
    foldev = pfold.enter_context(nc.sbuf_tensor("foldev", [128, NFS * D], BF16))
    p2v = contextlib.ExitStack()
    wvb = p2v.enter_context(nc.sbuf_tensor("wvb", [128, DT * D], BF16))
    wotb = p2v.enter_context(nc.sbuf_tensor("wotb", [128, DT * D], BF16, is_transpose_dest=True))
    wvoev = p2v.enter_context(nc.sbuf_tensor("wvoev", [128, 2 * D], BF16))

    sy.op("dma_start", out=cH128[:], in_=h128c).inc(Dcn)
    sy.op("dma_start", out=cBD[:], in_=bdc).inc(Dcn)
    sy.op("dma_start", out=fcb[:], in_=fconst).inc(Dcn)

    def cast(dst, src, r0, r1, nsplit, sm):
        step = (r1 - r0) // nsplit
        for rr in range(r0, r1, step):
            gp.op("dma_start", out=dst[rr:rr + step, :], in_=src[rr:rr + step, :]).inc(sm)

    cast(s_wq, wq, 0, R, 1, Dcq)
    cast(s_wk, wk, 0, R, 1, Dck)
    cast(s_wo, wo, 0, D, max(1, (D * D * 4) >> 23), Dco)

    for d1 in range(DT):
        gp.op("dma_start", out=wvb[:, d1 * D:(d1 + 1) * D],
              in_=wv[d1 * 128:(d1 + 1) * 128, :]).inc(Dwv)

    cast(s_w2, w2, 0, D, max(1, (D * H * 4) >> 23), Dc2)
    cast(s_w1, w1, 0, H // 2, max(1, (H * D * 2) >> 23), Dc1h[0])
    cast(s_w3, w3, 0, H // 2, max(1, (H * D * 2) >> 23), Dc3h[0])
    cast(s_w1, w1, H // 2, H, max(1, (H * D * 2) >> 23), Dc1h[1])
    cast(s_w3, w3, H // 2, H, max(1, (H * D * 2) >> 23), Dc3h[1])

    NST = max(1, D // 512)
    p1x = contextlib.ExitStack()
    xbuf = p1x.enter_context(nc.sbuf_tensor("xbuf", [128, 2 * D], F32))
    xnbuf = p1x.enter_context(nc.sbuf_tensor("xnbuf", [128, 2 * D], BF16))
    stat6 = p1x.enter_context(nc.sbuf_tensor("stat6", [128, 2 * 6 * NST], F32))
    mv2 = p1x.enter_context(nc.sbuf_tensor("mv2", [128, 4], F32))
    sd1 = p1x.enter_context(nc.sbuf_tensor("sd1", [128, 2], F32))
    rsb = p1x.enter_context(nc.sbuf_tensor("rsb", [128, 2], F32))
    nbb = p1x.enter_context(nc.sbuf_tensor("nbb", [128, 2], F32))

    p1_ve = [0] * TT
    p1_act = [0] * TT

    def p1_load(i):
        if i >= TT:
            return
        b = i % 2
        if i >= 2:
            sc.wait(Sve, p1_ve[i - 2])
        sc.op("dma_start", out=xbuf[:, b * D:(b + 1) * D],
              in_=x[i * 128:(i + 1) * 128, :]).inc(Dx[b])

    p1_load(0)
    p1_load(1)
    for i in range(TT):
        b = i % 2
        xs = xbuf[:, b * D:(b + 1) * D]
        ve.wait(Dx[b])
        if i >= 2:
            ve.wait(Ssc, p1_act[i - 2])
        for j in range(NST):
            w = min(512, D - j * 512)
            ve.op("bn_stats", out=stat6[:, (b * NST + j) * 6:(b * NST + j + 1) * 6],
                  in_=xs[:, j * 512:j * 512 + w]).inc(Sve)
        ve.wait(Sve)
        ve.op("bn_aggr", out=mv2[:, b * 2:b * 2 + 2],
              in_=stat6[:, b * NST * 6:(b + 1) * NST * 6]).inc(Sve)
        sc.wait(Sve)
        sc.wait(Dcn)
        sc.op("activation", out=sd1[:, b:b + 1], in_=mv2[:, b * 2 + 1:b * 2 + 2],
              func=AF.Sqrt, bias=fcb[:, 0:1]).inc(Ssc)
        ve.wait(Ssc)
        ve.op("reciprocal", out=rsb[:, b:b + 1], in_=sd1[:, b:b + 1]).inc(Sve)
        ve.wait(Sve)
        ve.op("scalar_tensor_tensor", out=nbb[:, b:b + 1], in0=mv2[:, b * 2:b * 2 + 1],
              scalar=-1.0, in1=rsb[:, b:b + 1], op0=ALU.mult, op1=ALU.mult).inc(Sve)
        p1_ve[i] = Sve.n
        sc.wait(Sve, p1_ve[i])
        sc.wait(Dx[b])
        sc.wait(Dxw[b])
        sc.op("activation", out=xnbuf[:, b * D:(b + 1) * D], in_=xs,
              func=AF.Identity, scale=rsb[:, b:b + 1], bias=nbb[:, b:b + 1]).inc(Ssc)
        p1_act[i] = Ssc.n
        sc.wait(Ssc, p1_act[i])
        sc.op("dma_start", out=s_xn[i * 128:(i + 1) * 128, :],
              in_=xnbuf[:, b * D:(b + 1) * D]).inc(Dxw[b])
        p1_load(i + 2)
    p1_sve, p1_ssc = Sve.n, Ssc.n
    p1x.close()

    p1_release = [(Sve, p1_sve), (Ssc, p1_ssc),
                  (Dx[0], Dx[0].n), (Dx[1], Dx[1].n),
                  (Dxw[0], Dxw[0].n), (Dxw[1], Dxw[1].n)]

    st1_te = [0] * HT
    st1_ev = [0] * HT
    st2_te = [0] * HT
    s1 = {"n": 0}
    s2 = {"n": 0}
    w2h_mark = []

    def pump_s1(k):
        for _ in range(k):
            a = s1["n"]
            if a >= HT:
                return
            s1["n"] += 1
            slot = a % NFS
            sy.wait(Dc2)
            if a >= NFS:
                sy.wait(Ste, st1_te[a - NFS])
            sy.op("dma_start", out=w2tb[:, slot * D:(slot + 1) * D],
                  in_=s_w2[:, a * 128:(a + 1) * 128], transpose=True).inc(Dw2t[slot])
            te.wait(Dw2t[slot])
            te.wait(Dcn)
            b = take()
            for (o, w) in _ch(D):
                st1_te[a] = te.op("matmul", PS[b][:, o:o + w], cH128[:],
                                  w2tb[:, slot * D + o:slot * D + o + w],
                                  start=True, stop=True).inc(Ste)
            ve.wait(Ste, st1_te[a])
            ve.wait(Dt1w[slot])
            sve_v = ve.op("tensor_copy", out=foldev[:, slot * D:(slot + 1) * D],
                          in_=PS[b][:, 0:D]).inc(Sve)
            st1_ev[a] = sve_v
            ps_rel[b] = [(Sve, sve_v)]
            sc.wait(Sve, sve_v)
            sc.op("dma_start", out=t1_swap[a],
                  in_=foldev[:, slot * D:(slot + 1) * D]).inc(Dt1w[slot])

    def pump_s2(k):
        for _ in range(k):
            t = s2["n"]
            if t >= HT:
                return
            s2["n"] += 1
            slot = t % NFS
            if t == 0:
                assert s1["n"] == HT
                for s_ in Dt1w:
                    sc.wait(s_)
                sc.wait(Ste, max(st1_te))
                sc.wait(Sve, max(st1_ev))
            if t >= NFS:
                sc.wait(Ste, st2_te[t - NFS])
            sc.op("dma_start", out=w2tb[:, slot * D:(slot + 1) * D],
                  in_=s_t1[t * 128:(t + 1) * 128, :]).inc(Dw2u[slot])
            te.wait(Dw2u[slot])
            b = take()
            for (o, w) in _ch(D):
                st2_te[t] = te.op("matmul", PS[b][:, o:o + w], cBD[:],
                                  w2tb[:, slot * D + o:slot * D + o + w],
                                  start=True, stop=True).inc(Ste)
            sc.wait(Ste, st2_te[t])
            sc.wait(Dw2hw[slot])
            ssc_v = sc.op("activation", out=foldev[:, slot * D:(slot + 1) * D],
                          in_=PS[b][:, 0:D], func=AF.Identity,
                          scale=fcb[:, 1:2]).inc(Ssc)
            ps_rel[b] = [(Ssc, ssc_v)]
            gp.wait(Ssc, ssc_v)
            gp.op("dma_start", out=w2h_unswap[t],
                  in_=foldev[:, slot * D:(slot + 1) * D]).inc(Dw2hw[slot])
            if s2["n"] % GCT == 0:
                w2h_mark.append([(s_, s_.n) for s_ in Dw2hw])

    sy.wait(Dco)
    for d1 in range(DT):
        sy.op("dma_start", out=wotb[:, d1 * D:(d1 + 1) * D],
              in_=s_wo[:, d1 * 128:(d1 + 1) * 128], transpose=True).inc(Dwot)

    te.wait(Dwot)
    te.wait(Dwv)
    wvo_te = [0] * DT
    for d2t in range(DT):
        b = take()
        for d1 in range(DT):
            lhs = wotb[:, d1 * D + d2t * 128: d1 * D + (d2t + 1) * 128]
            for (o, w) in _ch(D):
                hh = te.op("matmul", PS[b][:, o:o + w], lhs,
                           wvb[:, d1 * D + o:d1 * D + o + w],
                           start=(d1 == 0), stop=(d1 == DT - 1))
                if d1 == DT - 1:
                    wvo_te[d2t] = hh.inc(Ste)
        ve.wait(Ste, wvo_te[d2t])
        if d2t >= 2:
            ve.wait(Dvw[d2t % 2])
        sve_v = ve.op("tensor_copy", out=wvoev[:, (d2t % 2) * D:(d2t % 2 + 1) * D],
                      in_=PS[b][:, 0:D]).inc(Sve)
        ps_rel[b] = [(Sve, sve_v)]
        gp.wait(Sve, sve_v)
        gp.op("dma_start", out=s_wvoT[d2t * 128:(d2t + 1) * 128, :],
              in_=wvoev[:, (d2t % 2) * D:(d2t % 2 + 1) * D]).inc(Dvw[d2t % 2])
        pump_s1(max(1, (HT + DT - 1) // DT))
    pump_s1(HT)
    p3_te_end = Ste.n
    p3_ve_end = Sve.n

    p2v_release = [(Ste, p3_te_end), (Sve, p3_ve_end),
                   (Dvw[0], Dvw[0].n), (Dvw[1], Dvw[1].n), (Dwv, Dwv.n)]
    p2v.close()

    p46 = contextlib.ExitStack()
    xnT = p46.enter_context(nc.sbuf_tensor("xnT", [128, DT * T], BF16, is_transpose_dest=True))
    wqkT = p46.enter_context(nc.sbuf_tensor("wqkT", [128, 2 * DT * R], BF16, is_transpose_dest=True))
    qfT = p46.enter_context(nc.sbuf_tensor("qfT", [R, T], BF16))
    kfT = p46.enter_context(nc.sbuf_tensor("kfT", [R, T], BF16))

    boundary(p2v_release + p1_release)

    sy.wait(Dxw[0])
    sy.wait(Dxw[1])
    for d in range(DT):
        sy.op("dma_start", out=xnT[:, d * T:(d + 1) * T],
              in_=s_xn[:, d * 128:(d + 1) * 128], transpose=True).inc(DxnT)
    sy.wait(Dcq)
    sy.wait(Dck)
    for d in range(DT):
        sy.op("dma_start", out=wqkT[:, d * R:(d + 1) * R],
              in_=s_wq[:, d * 128:(d + 1) * 128], transpose=True).inc(Dwqk)
        sy.op("dma_start", out=wqkT[:, (DT + d) * R:(DT + d + 1) * R],
              in_=s_wk[:, d * 128:(d + 1) * 128], transpose=True).inc(Dwqk)
    wvob = p46.enter_context(nc.sbuf_tensor("wvob", [128, DT * D], BF16, is_transpose_dest=True))
    sy.wait(Dvw[0])
    sy.wait(Dvw[1])
    for d in range(DT):
        sy.op("dma_start", out=wvob[:, d * D:(d + 1) * D],
              in_=s_wvoT[:, d * 128:(d + 1) * 128], transpose=True).inc(Dwvo)

    HH = H // 2
    pret_half_mark = [None, None]
    pret_state = {"it": 0}

    def pump_pret(k):
        for _ in range(k):
            it = pret_state["it"]
            if it >= 4 * DT:
                return
            pret_state["it"] += 1
            hf = it // (2 * DT)
            rem = it % (2 * DT)
            d = rem // 2
            (s_src, d_wr, dcast) = [(s_w1, w1q_wr, Dc1h[hf]), (s_w3, w3q_wr, Dc3h[hf])][rem % 2]
            slot = 2 * hf + (it % 2)
            sy.wait(dcast)
            if hf == 1:
                sy.wait(Dptw[slot - 2])
            sy.wait(Dptw[slot])
            sy.op("dma_start", out=ptb[:, (slot % 2) * HH:(slot % 2 + 1) * HH],
                  in_=s_src[hf * HH:(hf + 1) * HH, d * 128:(d + 1) * 128],
                  transpose=True).inc(Dpt[slot])
            gp.wait(Dpt[slot])
            gp.op("dma_start", out=d_wr[:, d, hf * shf:(hf + 1) * shf, :],
                  in_=ptb[:, (slot % 2) * HH:(slot % 2 + 1) * HH]).inc(Dptw[slot])
            if pret_state["it"] == 2 * DT:
                pret_half_mark[0] = (Dptw[0].n, Dptw[1].n)
            if pret_state["it"] == 4 * DT:
                pret_half_mark[1] = (Dptw[2].n, Dptw[3].n)

    te.wait(DxnT)
    te.wait(Dwqk)
    for (o, w) in _ch(T):
        b0 = take()
        b1 = take()
        for d in range(DT):
            h0 = te.op("matmul", PS[b0][0:R, o:o + w], wqkT[:, d * R:(d + 1) * R],
                       xnT[:, d * T + o:d * T + o + w], start=(d == 0), stop=(d == DT - 1))
        h0.inc(Ste)
        for d in range(DT):
            h1 = te.op("matmul", PS[b1][0:R, o:o + w], wqkT[:, (DT + d) * R:(DT + d + 1) * R],
                       xnT[:, d * T + o:d * T + o + w], start=(d == 0), stop=(d == DT - 1))
        qk_te = h1.inc(Ste)
        sc.wait(Ste, qk_te)
        sc.op("activation", out=qfT[:, o:o + w], in_=PS[b0][0:R, o:o + w], func=AF.Copy).inc(Ssc)
        qk_sc = sc.op("activation", out=kfT[:, o:o + w], in_=PS[b1][0:R, o:o + w],
                      func=AF.Copy).inc(Ssc)
        ps_rel[b0] = [(Ssc, qk_sc)]
        ps_rel[b1] = [(Ssc, qk_sc)]
        pump_s2(4)
        pump_pret(4)
    qk_done_sc = Ssc.n

    abuf = p46.enter_context(nc.sbuf_tensor("abuf", [128, 2 * T], BF16))
    negm = p46.enter_context(nc.sbuf_tensor("negm", [128, 2], F32))
    sumb = p46.enter_context(nc.sbuf_tensor("sumb", [128, 2], F32))

    te.wait(Ssc, qk_done_sc)
    a_sc = [0] * TT
    for i in range(TT):
        bs = i % 2
        b = take()
        s_te = 0
        for (o, w) in _ch(T):
            s_te = te.op("matmul", PS[b][:, o:o + w], qfT[:, i * 128:(i + 1) * 128],
                         kfT[:, o:o + w], start=True, stop=True).inc(Ste)
        ve.wait(Ste, s_te)
        if i >= 2:
            ve.wait(Ssc, a_sc[i - 2])
        negm_v = ve.op("reduce_max", out=negm[:, bs:bs + 1], in_=PS[b][:, 0:T], axis=AXX,
                       negate=True).inc(Sve)
        sc.wait(Sve, negm_v)
        sc.wait(Daw[bs])
        a_sc[i] = sc.op("activation", out=abuf[:, bs * T:(bs + 1) * T], in_=PS[b][:, 0:T],
                        func=AF.Exp, bias=negm[:, bs:bs + 1]).inc(Ssc)
        ps_rel[b] = [(Ssc, a_sc[i]), (Sve, negm_v)]
        ve.wait(Ssc, a_sc[i])
        ve.op("reduce_sum", out=sumb[:, bs:bs + 1], in_=abuf[:, bs * T:(bs + 1) * T],
              axis=AXX).inc(Sve)
        ve.wait(Sve)
        ve.op("reciprocal", out=recipb[:, i:i + 1], in_=sumb[:, bs:bs + 1]).inc(Sve)
        gp.wait(Ssc, a_sc[i])
        gp.op("dma_start", out=s_a[i * 128:(i + 1) * 128, :],
              in_=abuf[:, bs * T:(bs + 1) * T]).inc(Daw[bs])
        pump_s2(2)
        pump_pret(7)
    recip_done = Sve.n

    uev = p46.enter_context(nc.sbuf_tensor("uev", [128, 2 * D], BF16))
    te.wait(Dwvo)
    for i in range(TT):
        bs = i % 2
        b = take()
        u_te = 0
        for d in range(DT):
            lhs = xnT[:, d * T + i * 128: d * T + (i + 1) * 128]
            for (o, w) in _ch(D):
                hh = te.op("matmul", PS[b][:, o:o + w], lhs,
                           wvob[:, d * D + o:d * D + o + w],
                           start=(d == 0), stop=(d == DT - 1))
                if d == DT - 1:
                    u_te = hh.inc(Ste)
        sc.wait(Ste, u_te)
        sc.wait(Duw[bs])
        ssc_v = sc.op("activation", out=uev[:, bs * D:(bs + 1) * D], in_=PS[b][:, 0:D],
                      func=AF.Copy).inc(Ssc)
        ps_rel[b] = [(Ssc, ssc_v)]
        gp.wait(Ssc, ssc_v)
        gp.op("dma_start", out=s_u[i * 128:(i + 1) * 128, :],
              in_=uev[:, bs * D:(bs + 1) * D]).inc(Duw[bs])
        pump_s2(2)
    pump_s2(HT)
    assert s2["n"] == HT and len(w2h_mark) == NGRP

    pump_pret(4 * DT)
    assert pret_state["it"] == 4 * DT

    p46_te_end = Ste.n
    p46.close()
    pfold_release = [(Ste, max(max(st2_te), 1)),
                     *[(s_, s_.n) for s_ in Dw2u], *[(s_, s_.n) for s_ in Dw2hw]]
    pfold.close()

    p7 = contextlib.ExitStack()
    xnTh = p7.enter_context(nc.sbuf_tensor("xnTh", [128, DT * TH], BF16, is_transpose_dest=True))
    acch = p7.enter_context(nc.sbuf_tensor("acch", [128, THT * D], BF16))
    gbuf = p7.enter_context(nc.sbuf_tensor("gbuf", [128, GCT * TH], BF16))
    w1b = p7.enter_context(nc.sbuf_tensor("w1b", [128, 2 * DT * SCW], BF16))
    w3b = p7.enter_context(nc.sbuf_tensor("w3b", [128, 2 * DT * SCW], BF16))
    w2hb = p7.enter_context(nc.sbuf_tensor("w2hb", [128, GCT * D], BF16))
    silt = p7.enter_context(nc.sbuf_tensor("silt", [128, 2 * TH], BF16))

    boundary([(Ste, p46_te_end), (Sve, Sve.n), (Ssc, Ssc.n),
              (Daw[0], Daw[0].n), (Daw[1], Daw[1].n),
              (Duw[0], Duw[0].n), (Duw[1], Duw[1].n),
              *pfold_release])

    sil_guard = [0, 0]
    acc_spill = [0, 0]
    for half in range(2):
        r0h = half * TH
        for d in range(DT):
            sy.op("dma_start", out=xnTh[:, d * TH:(d + 1) * TH],
                  in_=s_xn[r0h:r0h + TH, d * 128:(d + 1) * 128], transpose=True).inc(DxnTh)
        xnTh_done = DxnTh.n

        sup_te = [0] * NSUP

        def p7_load_super(s, _sup_te=sup_te):
            if s >= NSUP:
                return
            bsl = s % 2
            hf = s // shf
            m0, m1 = pret_half_mark[hf]
            sy.wait(Dptw[2 * hf], m0)
            sy.wait(Dptw[2 * hf + 1], m1)
            gp.wait(Dptw[2 * hf], m0)
            gp.wait(Dptw[2 * hf + 1], m1)
            if s >= 2:
                sy.wait(Ste, _sup_te[s - 2])
                gp.wait(Ste, _sup_te[s - 2])
            sy.op("dma_start", out=w1b[:, bsl * DT * SCW:(bsl + 1) * DT * SCW],
                  in_=w1q_sup[s]).inc(Dw1[bsl])
            gp.op("dma_start", out=w3b[:, bsl * DT * SCW:(bsl + 1) * DT * SCW],
                  in_=w3q_sup[s]).inc(Dw3[bsl])

        p7_load_super(0)
        p7_load_super(1)

        te.wait(DxnTh, xnTh_done)
        acc_ve = [0] * THT
        for g in range(NGRP):
            gp.wait(Ste)
            for s_, v in w2h_mark[g]:
                gp.wait(s_, v)
            gp.op("dma_start", out=w2hb[:, 0:GCT * D],
                  in_=w2h_rows[:, g * GCT:(g + 1) * GCT, :]).inc(Dw2hb)

            g_mul = 0
            for si in range(SUPG):
                s = g * SUPG + si
                bsl = s % 2
                te.wait(Dw1[bsl])
                te.wait(Dw3[bsl])
                for ct in range(SCT):
                    c_glob = s * SCT + ct
                    cg = c_glob - g * GCT
                    slot = c_glob % 2
                    pb = take()
                    gate_te = lin_te = 0
                    for d in range(DT):
                        lhs = w1b[:, (bsl * DT + d) * SCW + ct * 128:(bsl * DT + d) * SCW + (ct + 1) * 128]
                        for (o, w) in _ch(TH):
                            hh = te.op("matmul", PS[pb][:, o:o + w], lhs,
                                       xnTh[:, d * TH + o:d * TH + o + w],
                                       start=(d == 0), stop=(d == DT - 1))
                            if d == DT - 1:
                                gate_te = hh.inc(Ste)
                    for d in range(DT):
                        lhs = w3b[:, (bsl * DT + d) * SCW + ct * 128:(bsl * DT + d) * SCW + (ct + 1) * 128]
                        for (o, w) in _ch(TH):
                            hh = te.op("matmul", PS[pb][:, 1024 + o:1024 + o + w], lhs,
                                       xnTh[:, d * TH + o:d * TH + o + w],
                                       start=(d == 0), stop=(d == DT - 1))
                            if d == DT - 1:
                                lin_te = hh.inc(Ste)
                    sc.wait(Ste, gate_te)
                    sc.wait(Sve, sil_guard[slot])
                    sil_sc = sc.op("activation", out=silt[:, slot * TH:(slot + 1) * TH],
                                   in_=PS[pb][:, 0:TH], func=AF.Silu).inc(Ssc)
                    ve.wait(Ssc, sil_sc)
                    ve.wait(Ste, lin_te)
                    g_mul = ve.op("tensor_tensor", out=gbuf[:, cg * TH:(cg + 1) * TH],
                                  in0=silt[:, slot * TH:(slot + 1) * TH],
                                  in1=PS[pb][:, 1024:1024 + TH], op=ALU.mult).inc(Sve)
                    sil_guard[slot] = g_mul
                    ps_rel[pb] = [(Sve, g_mul), (Ssc, sil_sc)]
                sup_te[s] = Ste.n
                p7_load_super(s + 2)
            te.wait(Sve, g_mul)
            te.wait(Dw2hb)
            for j in range(THT):
                pb = take()
                dn_te = 0
                for ci in range(GCT):
                    lhs = gbuf[:, ci * TH + j * 128: ci * TH + (j + 1) * 128]
                    for (o, w) in _ch(D):
                        hh = te.op("matmul", PS[pb][:, o:o + w], lhs,
                                   w2hb[:, ci * D + o:ci * D + o + w],
                                   start=(ci == 0), stop=(ci == GCT - 1))
                        if ci == GCT - 1:
                            dn_te = hh.inc(Ste)
                ve.wait(Ste, dn_te)
                if g == 0:
                    if half == 1 and j == 0:
                        ve.wait(Daccw, acc_spill[0])
                    hh = ve.op("tensor_copy", out=acch[:, j * D:(j + 1) * D], in_=PS[pb][:, 0:D])
                else:
                    ve.wait(Sve, acc_ve[j])
                    hh = ve.op("tensor_tensor", out=acch[:, j * D:(j + 1) * D],
                               in0=PS[pb][:, 0:D], in1=acch[:, j * D:(j + 1) * D], op=ALU.add)
                acc_ve[j] = hh.inc(Sve)
                ps_rel[pb] = [(Sve, acc_ve[j])]
            if g + 1 < NGRP:
                ve.wait(Ste, Ste.n)
        for j in range(THT):
            gp.wait(Sve, acc_ve[j])
        gp.op("dma_start", out=acc_rows[:, half * THT:(half + 1) * THT, :],
              in_=acch[:, 0:THT * D]).inc(Daccw)
        acc_spill[half] = Daccw.n
        if half == 0:
            sy.wait(Ste, Ste.n)
            ve.wait(Ste, Ste.n)
    p7_te_end = Ste.n
    p7.close()
    p4t_release = [*[(s_, s_.n) for s_ in Dpt], *[(s_, s_.n) for s_ in Dptw]]
    p4t.close()

    p8 = contextlib.ExitStack()
    ubuf = p8.enter_context(nc.sbuf_tensor("ubuf", [128, DT * D], BF16))
    NQC = max(1, T // 512)
    QW = T // NQC
    QT = QW // 128
    atb = p8.enter_context(nc.sbuf_tensor("atb", [128, 2 * TT * QW], BF16, is_transpose_dest=True))
    xrb = p8.enter_context(nc.sbuf_tensor("xrb", [128, 2 * D], F32))
    accrb = p8.enter_context(nc.sbuf_tensor("accrb", [128, 2 * D], BF16))
    t2b = p8.enter_context(nc.sbuf_tensor("t2b", [128, 2 * D], F32))
    outb = p8.enter_context(nc.sbuf_tensor("outb", [128, 2 * D], F32))

    boundary([(Ste, p7_te_end), (Sve, Sve.n), (Ssc, Ssc.n), (Daccw, Daccw.n),
              *p4t_release])
    gp.wait(Duw[0])
    gp.wait(Duw[1])
    for tk in range(TT):
        gp.op("dma_start", out=ubuf[:, tk * D:(tk + 1) * D],
              in_=s_u[tk * 128:(tk + 1) * 128, :]).inc(Dub)

    at_te = [0] * NQC
    sy.wait(Daw[0])
    sy.wait(Daw[1])
    ve.wait(Sve, recip_done)

    def p8_load_at(q):
        if q >= NQC:
            return
        bq = q % 2
        if q >= 2:
            sy.wait(Ste, at_te[q - 2])
        for tk in range(TT):
            sy.op("dma_start", out=atb[:, (bq * TT + tk) * QW:(bq * TT + tk + 1) * QW],
                  in_=s_a[q * QW:(q + 1) * QW, tk * 128:(tk + 1) * 128],
                  transpose=True).inc(Dat[bq])

    p8_load_at(0)
    p8_load_at(1)
    te.wait(Dub)
    out_ve = [0] * TT

    def p8_load_xacc(i):
        if i >= TT:
            return
        bs = i % 2
        if i >= 2:
            sc.wait(Sve, out_ve[i - 2])
        sc.op("dma_start", out=xrb[:, bs * D:(bs + 1) * D],
              in_=x[i * 128:(i + 1) * 128, :]).inc(Dxa[bs])
        if i == 0:
            sc.wait(Daccw)
        sc.op("dma_start", out=accrb[:, bs * D:(bs + 1) * D],
              in_=s_acc[i * 128:(i + 1) * 128, :]).inc(Dxa[bs])

    p8_load_xacc(0)
    p8_load_xacc(1)
    for q in range(NQC):
        bq = q % 2
        te.wait(Dat[bq])
        for j in range(QT):
            i = q * QT + j
            bs = i % 2
            b = take()
            a_te = 0
            for tk in range(TT):
                lhs = atb[:, (bq * TT + tk) * QW + j * 128:(bq * TT + tk) * QW + (j + 1) * 128]
                for (o, w) in _ch(D):
                    hh = te.op("matmul", PS[b][:, o:o + w], lhs,
                               ubuf[:, tk * D + o:tk * D + o + w],
                               start=(tk == 0), stop=(tk == TT - 1))
                    if tk == TT - 1:
                        a_te = hh.inc(Ste)
            if j == QT - 1:
                at_te[q] = a_te
            ve.wait(Ste, a_te)
            ve.wait(Dxa[bs])
            ve.wait(Dow[bs])
            ve.op("tensor_tensor", out=t2b[:, bs * D:(bs + 1) * D],
                  in0=xrb[:, bs * D:(bs + 1) * D], in1=accrb[:, bs * D:(bs + 1) * D],
                  op=ALU.add).inc(Sve)
            ve.wait(Sve)
            out_ve[i] = ve.op("scalar_tensor_tensor", out=outb[:, bs * D:(bs + 1) * D],
                              in0=PS[b][:, 0:D], scalar=recipb[:, i:i + 1],
                              in1=t2b[:, bs * D:(bs + 1) * D],
                              op0=ALU.mult, op1=ALU.add).inc(Sve)
            ps_rel[b] = [(Sve, out_ve[i])]
            gp.wait(Sve, out_ve[i])
            gp.op("dma_start", out=out[i * 128:(i + 1) * 128, :],
                  in_=outb[:, bs * D:(bs + 1) * D]).inc(Dow[bs])
            p8_load_xacc(i + 2)
        p8_load_at(q + 2)
    gp.wait(Dow[0])
    gp.wait(Dow[1])
    p8.close()

    with nc.Block() as block:
        @block.sync
        def _(eng):
            sy.play(eng)

        @block.scalar
        def _(eng):
            sc.play(eng)

        @block.vector
        def _(eng):
            ve.play(eng)

        @block.tensor
        def _(eng):
            te.play(eng)

        @block.gpsimd
        def _(eng):
            gp.play(eng)

    stack.close()
    return nc


def kernel(L0_attn_hidden_states, L0_attn_Wq, L0_attn_Wk, L0_attn_Wv, L0_attn_Wo,
           L0_ffn_W1, L0_ffn_W2, L0_ffn_W3):
    x = np.ascontiguousarray(np.asarray(L0_attn_hidden_states, dtype=np.float32))
    B, T, D = x.shape
    H = L0_ffn_W1.shape[0]
    R = L0_attn_Wq.shape[0]
    nc = build_program(T, D, H, R)
    h128, bd = make_consts(H)
    common = {
        "wq": np.ascontiguousarray(np.asarray(L0_attn_Wq, np.float32)),
        "wk": np.ascontiguousarray(np.asarray(L0_attn_Wk, np.float32)),
        "wv": np.ascontiguousarray(np.asarray(L0_attn_Wv, np.float32)),
        "wo": np.ascontiguousarray(np.asarray(L0_attn_Wo, np.float32)),
        "w1": np.ascontiguousarray(np.asarray(L0_ffn_W1, np.float32)),
        "w2": np.ascontiguousarray(np.asarray(L0_ffn_W2, np.float32)),
        "w3": np.ascontiguousarray(np.asarray(L0_ffn_W3, np.float32)),
        "h128": h128,
        "bd": bd,
        "fconst": np.tile(np.array([[EPS, 1.0 / np.sqrt(H)]], np.float32), (128, 1)),
    }
    in_maps = [{"x": np.ascontiguousarray(x[i]), **common} for i in range(B)]
    res = run_bass_kernel_spmd(nc, in_maps, core_ids=list(range(B)))
    return np.stack([np.asarray(r["out"], dtype=np.float32) for r in res.results], axis=0)


# version 12
# speedup vs baseline: 1.1583x; 1.1152x over previous
import contextlib
import numpy as np
import ml_dtypes
import concourse.bass as bass
import concourse.mybir as mybir
from concourse.bass_utils import run_bass_kernel_spmd

F32 = mybir.dt.float32
BF16 = mybir.dt.bfloat16
AXX = mybir.AxisListType.X
AF = mybir.ActivationFunctionType
ALU = mybir.AluOpType
BF = ml_dtypes.bfloat16
EPS = 1e-5


def _hadamard(n):
    h = np.array([[1.0]], dtype=np.float64)
    while h.shape[0] < n:
        h = np.block([[h, h], [h, -h]])
    return h


def make_consts(H):
    NHI = H // 128
    E = 128 // NHI
    h128 = _hadamard(128).astype(BF)
    bd = np.kron(np.eye(E), _hadamard(NHI)).astype(BF)
    return h128, bd


class _Sem:
    def __init__(self, h, name):
        self.h = h
        self.name = name
        self.n = 0


class _Rec:
    def __init__(self, name):
        self.name = name
        self.ops = []
        self._hwm = {}

    def wait(self, sem, val=None):
        v = sem.n if val is None else val
        if v <= 0 or self._hwm.get(sem.name, -1) >= v:
            return
        self._hwm[sem.name] = v
        self.ops.append(("wait_ge", (sem, v), {}, []))

    def op(self, meth, *a, **k):
        entry = [meth, a, k, []]
        self.ops.append(entry)

        class _H:
            def inc(_s, sem, amt=None):
                d = 16 if meth == "dma_start" else 1
                if amt is not None:
                    d = amt
                sem.n += d
                entry[3].append((sem, d))
                return sem.n
        return _H()

    def play(self, eng):
        for meth, a, k, incs in self.ops:
            if meth == "wait_ge":
                eng.wait_ge(a[0].h, a[1])
                continue
            ins = getattr(eng, meth)(*a, **k)
            for sem, d in incs:
                ins = ins.then_inc(sem.h, d)


def _ch(total, step=512):
    return [(o, min(step, total - o)) for o in range(0, total, step)]


def build_program(T, D, H, R=64):
    DT, TT, HT = D // 128, T // 128, H // 128
    NHI = HT
    E = 128 // NHI
    assert NHI * 128 == H and E * NHI == 128
    TH = T // 2
    THT = TH // 128
    SCW = min(512, H)
    SCT = SCW // 128
    NSUP = H // SCW
    GCT = min(8, HT)
    NGRP = HT // GCT
    SUPG = GCT // SCT
    shf = NSUP // 2
    NFS = 3

    nc = bass.Bass("TRN2", target_bir_lowering=False)

    x = nc.dram_tensor("x", [T, D], F32, kind="ExternalInput").ap()
    wq = nc.dram_tensor("wq", [R, D], F32, kind="ExternalInput").ap()
    wk = nc.dram_tensor("wk", [R, D], F32, kind="ExternalInput").ap()
    wv = nc.dram_tensor("wv", [D, D], F32, kind="ExternalInput").ap()
    wo = nc.dram_tensor("wo", [D, D], F32, kind="ExternalInput").ap()
    w1 = nc.dram_tensor("w1", [H, D], F32, kind="ExternalInput").ap()
    w2 = nc.dram_tensor("w2", [D, H], F32, kind="ExternalInput").ap()
    w3 = nc.dram_tensor("w3", [H, D], F32, kind="ExternalInput").ap()
    h128c = nc.dram_tensor("h128", [128, 128], BF16, kind="ExternalInput").ap()
    bdc = nc.dram_tensor("bd", [128, 128], BF16, kind="ExternalInput").ap()
    fconst = nc.dram_tensor("fconst", [128, 2], F32, kind="ExternalInput").ap()
    identc = nc.dram_tensor("identc", [128, 128], F32, kind="ExternalInput").ap()
    out = nc.dram_tensor("out", [T, D], F32, kind="ExternalOutput").ap()

    s_wq = nc.dram_tensor("s_wq", [R, D], BF16, kind="Internal").ap()
    s_wk = nc.dram_tensor("s_wk", [R, D], BF16, kind="Internal").ap()
    s_wo = nc.dram_tensor("s_wo", [D, D], BF16, kind="Internal").ap()
    s_w2 = nc.dram_tensor("s_w2", [D, H], BF16, kind="Internal").ap()
    s_xn = nc.dram_tensor("s_xn", [T, D], BF16, kind="Internal").ap()
    s_t1 = nc.dram_tensor("s_t1", [H, D], BF16, kind="Internal").ap()
    s_w2h = nc.dram_tensor("s_w2h", [H, D], BF16, kind="Internal").ap()
    s_wvoT = nc.dram_tensor("s_wvoT", [D, D], BF16, kind="Internal").ap()
    s_u = nc.dram_tensor("s_u", [T, D], BF16, kind="Internal").ap()
    s_a = nc.dram_tensor("s_a", [T, T], BF16, kind="Internal").ap()
    s_acc = nc.dram_tensor("s_acc", [T, D], BF16, kind="Internal").ap()
    s_w1q = nc.dram_tensor("s_w1q", [NSUP * 128 * DT * SCW], BF16, kind="Internal").ap()
    s_w3q = nc.dram_tensor("s_w3q", [NSUP * 128 * DT * SCW], BF16, kind="Internal").ap()

    t1_swap = s_t1.rearrange("(b a) d -> a b d", a=NHI)
    w2h_unswap = s_w2h.rearrange("(m tau e) d -> tau e m d", m=NHI, e=E)
    w2h_rows = s_w2h.rearrange("(n p) d -> p n d", p=128)
    acc_rows = s_acc.rearrange("(n p) d -> p n d", p=128)
    w1q_cb = s_w1q.rearrange("(cb p f) -> cb p f", cb=HT, p=128)
    w3q_cb = s_w3q.rearrange("(cb p f) -> cb p f", cb=HT, p=128)

    sy, sc, ve, te, gp = (_Rec(n) for n in ("sy", "sc", "ve", "te", "gp"))
    stack = contextlib.ExitStack()

    def sem(name):
        return _Sem(stack.enter_context(nc.semaphore(name)), name)

    Ste, Ssc, Sve = sem("Ste"), sem("Ssc"), sem("Sve")
    Dcn = sem("Dcn")
    Dcq, Dck, Dco, Dc2 = sem("Dcq"), sem("Dck"), sem("Dco"), sem("Dc2")
    Dwl = [sem(f"Dwl{i}") for i in range(2)]
    Dww = [sem(f"Dww{i}") for i in range(2)]
    Dx = [sem("Dx0"), sem("Dx1")]
    Dxw = [sem("Dxw0"), sem("Dxw1")]
    Dw2t = [sem(f"Dw2t{i}") for i in range(NFS)]
    Dt1w = [sem(f"Dt1w{i}") for i in range(NFS)]
    Dw2u = [sem(f"Dw2u{i}") for i in range(NFS)]
    Dw2hw = [sem(f"Dw2hw{i}") for i in range(NFS)]
    Dwot = sem("Dwot")
    Dwv = sem("Dwv")
    Dvw = [sem("Dvw0"), sem("Dvw1")]
    DxnT = sem("DxnT")
    Dwqk = sem("Dwqk")
    Daw = [sem("Daw0"), sem("Daw1")]
    Dwvo = sem("Dwvo")
    Duw = [sem("Duw0"), sem("Duw1")]
    DxnTh = sem("DxnTh")
    Dw1 = [sem("Dw10"), sem("Dw11")]
    Dw3 = [sem("Dw30"), sem("Dw31")]
    Dw2hb = sem("Dw2hb")
    Daccw = sem("Daccw")
    Dub = sem("Dub")
    Dat = [sem("Dat0"), sem("Dat1")]
    Dxa = [sem("Dxa0"), sem("Dxa1")]
    Dow = [sem("Dow0"), sem("Dow1")]

    cH128 = stack.enter_context(nc.sbuf_tensor("cH128", [128, 128], BF16))
    cI = stack.enter_context(nc.sbuf_tensor("cI", [128, 128], F32))
    cBD = stack.enter_context(nc.sbuf_tensor("cBD", [128, 128], BF16))
    recipb = stack.enter_context(nc.sbuf_tensor("recipb", [128, TT], F32))
    fcb = stack.enter_context(nc.sbuf_tensor("fcb", [128, 2], F32))

    ps0 = stack.enter_context(nc.psum_tensor("ps0", [128, 2048], F32))
    ps1 = stack.enter_context(nc.psum_tensor("ps1", [128, 2048], F32))
    PS = [ps0, ps1]
    ps_rel = [[], []]
    srr = [0]

    def take():
        b = srr[0] % 2
        srr[0] += 1
        for s_, v in ps_rel[b]:
            te.wait(s_, v)
        ps_rel[b] = []
        return b

    def boundary(release, recs=None):
        for rec in (recs or (sy, sc, ve, gp)):
            for s_, v in release:
                rec.wait(s_, v)

    p4t = contextlib.ExitStack()
    wtb = p4t.enter_context(nc.sbuf_tensor("wtb", [128, 2 * D], F32))
    wtev = p4t.enter_context(nc.sbuf_tensor("wtev", [128, 2 * D], BF16))
    pfold = contextlib.ExitStack()
    w2tb = pfold.enter_context(nc.sbuf_tensor("w2tb", [128, NFS * D], BF16, is_transpose_dest=True))
    foldev = pfold.enter_context(nc.sbuf_tensor("foldev", [128, NFS * D], BF16))
    p2v = contextlib.ExitStack()
    wvb = p2v.enter_context(nc.sbuf_tensor("wvb", [128, DT * D], BF16))
    wotb = p2v.enter_context(nc.sbuf_tensor("wotb", [128, DT * D], BF16, is_transpose_dest=True))
    wvoev = p2v.enter_context(nc.sbuf_tensor("wvoev", [128, D], BF16))

    sy.op("dma_start", out=cH128[:], in_=h128c).inc(Dcn)
    sy.op("dma_start", out=cBD[:], in_=bdc).inc(Dcn)
    sy.op("dma_start", out=fcb[:], in_=fconst).inc(Dcn)
    sy.op("dma_start", out=cI[:], in_=identc).inc(Dcn)

    def cast(dst, src, r0, r1, nsplit, sm):
        step = (r1 - r0) // nsplit
        for rr in range(r0, r1, step):
            gp.op("dma_start", out=dst[rr:rr + step, :], in_=src[rr:rr + step, :]).inc(sm)

    cast(s_wq, wq, 0, R, 1, Dcq)
    cast(s_wk, wk, 0, R, 1, Dck)
    cast(s_wo, wo, 0, D, max(1, (D * D * 4) >> 23), Dco)

    for d1 in range(DT):
        gp.op("dma_start", out=wvb[:, d1 * D:(d1 + 1) * D],
              in_=wv[d1 * 128:(d1 + 1) * 128, :]).inc(Dwv)

    cast(s_w2, w2, 0, D, max(1, (D * H * 4) >> 23), Dc2)

    NST = max(1, D // 512)
    p1x = contextlib.ExitStack()
    xbuf = p1x.enter_context(nc.sbuf_tensor("xbuf", [128, 2 * D], F32))
    xnbuf = p1x.enter_context(nc.sbuf_tensor("xnbuf", [128, 2 * D], BF16))
    stat6 = p1x.enter_context(nc.sbuf_tensor("stat6", [128, 2 * 6 * NST], F32))
    mv2 = p1x.enter_context(nc.sbuf_tensor("mv2", [128, 4], F32))
    sd1 = p1x.enter_context(nc.sbuf_tensor("sd1", [128, 2], F32))
    rsb = p1x.enter_context(nc.sbuf_tensor("rsb", [128, 2], F32))
    nbb = p1x.enter_context(nc.sbuf_tensor("nbb", [128, 2], F32))

    p1_ve = [0] * TT
    p1_act = [0] * TT

    def p1_load(i):
        if i >= TT:
            return
        b = i % 2
        if i >= 2:
            sc.wait(Sve, p1_ve[i - 2])
        sc.op("dma_start", out=xbuf[:, b * D:(b + 1) * D],
              in_=x[i * 128:(i + 1) * 128, :]).inc(Dx[b])

    p1_load(0)
    p1_load(1)
    for i in range(TT):
        b = i % 2
        xs = xbuf[:, b * D:(b + 1) * D]
        ve.wait(Dx[b])
        if i >= 2:
            ve.wait(Ssc, p1_act[i - 2])
        for j in range(NST):
            w = min(512, D - j * 512)
            ve.op("bn_stats", out=stat6[:, (b * NST + j) * 6:(b * NST + j + 1) * 6],
                  in_=xs[:, j * 512:j * 512 + w]).inc(Sve)
        ve.wait(Sve)
        ve.op("bn_aggr", out=mv2[:, b * 2:b * 2 + 2],
              in_=stat6[:, b * NST * 6:(b + 1) * NST * 6]).inc(Sve)
        sc.wait(Sve)
        sc.wait(Dcn)
        sc.op("activation", out=sd1[:, b:b + 1], in_=mv2[:, b * 2 + 1:b * 2 + 2],
              func=AF.Sqrt, bias=fcb[:, 0:1]).inc(Ssc)
        ve.wait(Ssc)
        ve.op("reciprocal", out=rsb[:, b:b + 1], in_=sd1[:, b:b + 1]).inc(Sve)
        ve.wait(Sve)
        ve.op("scalar_tensor_tensor", out=nbb[:, b:b + 1], in0=mv2[:, b * 2:b * 2 + 1],
              scalar=-1.0, in1=rsb[:, b:b + 1], op0=ALU.mult, op1=ALU.mult).inc(Sve)
        p1_ve[i] = Sve.n
        sc.wait(Sve, p1_ve[i])
        sc.wait(Dx[b])
        sc.wait(Dxw[b])
        sc.op("activation", out=xnbuf[:, b * D:(b + 1) * D], in_=xs,
              func=AF.Identity, scale=rsb[:, b:b + 1], bias=nbb[:, b:b + 1]).inc(Ssc)
        p1_act[i] = Ssc.n
        sc.wait(Ssc, p1_act[i])
        sc.op("dma_start", out=s_xn[i * 128:(i + 1) * 128, :],
              in_=xnbuf[:, b * D:(b + 1) * D]).inc(Dxw[b])
        p1_load(i + 2)
    p1_sve, p1_ssc = Sve.n, Ssc.n
    p1x.close()

    p1_release = [(Sve, p1_sve), (Ssc, p1_ssc),
                  (Dx[0], Dx[0].n), (Dx[1], Dx[1].n),
                  (Dxw[0], Dxw[0].n), (Dxw[1], Dxw[1].n)]

    st1_te = [0] * HT
    st1_ev = [0] * HT
    st2_te = [0] * HT
    s1 = {"n": 0}
    s2 = {"n": 0}
    w2h_mark = []

    def pump_s1(k):
        for _ in range(k):
            a = s1["n"]
            if a >= HT:
                return
            s1["n"] += 1
            slot = a % NFS
            sy.wait(Dc2)
            if a >= NFS:
                sy.wait(Ste, st1_te[a - NFS])
            sy.op("dma_start", out=w2tb[:, slot * D:(slot + 1) * D],
                  in_=s_w2[:, a * 128:(a + 1) * 128], transpose=True).inc(Dw2t[slot])
            te.wait(Dw2t[slot])
            te.wait(Dcn)
            b = take()
            for (o, w) in _ch(D):
                st1_te[a] = te.op("matmul", PS[b][:, o:o + w], cH128[:],
                                  w2tb[:, slot * D + o:slot * D + o + w],
                                  start=True, stop=True).inc(Ste)
            ve.wait(Ste, st1_te[a])
            ve.wait(Dt1w[slot])
            sve_v = ve.op("tensor_copy", out=foldev[:, slot * D:(slot + 1) * D],
                          in_=PS[b][:, 0:D]).inc(Sve)
            st1_ev[a] = sve_v
            ps_rel[b] = [(Sve, sve_v)]
            sc.wait(Sve, sve_v)
            sc.op("dma_start", out=t1_swap[a],
                  in_=foldev[:, slot * D:(slot + 1) * D]).inc(Dt1w[slot])

    def pump_s2(k):
        for _ in range(k):
            t = s2["n"]
            if t >= HT:
                return
            s2["n"] += 1
            slot = t % NFS
            if t == 0:
                assert s1["n"] == HT
                for s_ in Dt1w:
                    sc.wait(s_)
                sc.wait(Ste, max(st1_te))
                sc.wait(Sve, max(st1_ev))
            if t >= NFS:
                sc.wait(Ste, st2_te[t - NFS])
            sc.op("dma_start", out=w2tb[:, slot * D:(slot + 1) * D],
                  in_=s_t1[t * 128:(t + 1) * 128, :]).inc(Dw2u[slot])
            te.wait(Dw2u[slot])
            b = take()
            for (o, w) in _ch(D):
                st2_te[t] = te.op("matmul", PS[b][:, o:o + w], cBD[:],
                                  w2tb[:, slot * D + o:slot * D + o + w],
                                  start=True, stop=True).inc(Ste)
            sc.wait(Ste, st2_te[t])
            sc.wait(Dw2hw[slot])
            ssc_v = sc.op("activation", out=foldev[:, slot * D:(slot + 1) * D],
                          in_=PS[b][:, 0:D], func=AF.Identity,
                          scale=fcb[:, 1:2]).inc(Ssc)
            ps_rel[b] = [(Ssc, ssc_v)]
            gp.wait(Ssc, ssc_v)
            gp.op("dma_start", out=w2h_unswap[t],
                  in_=foldev[:, slot * D:(slot + 1) * D]).inc(Dw2hw[slot])
            if s2["n"] % GCT == 0:
                w2h_mark.append([(s_, s_.n) for s_ in Dw2hw])

    wt_state = {"n": 0}
    NWT = 2 * HT
    wt_te = [0] * NWT
    wt_ev = [0] * NWT
    wt_mark = [None] * NWT

    def pump_wt(k):
        for _ in range(k):
            n = wt_state["n"]
            if n >= NWT:
                return
            wt_state["n"] += 1
            mat = n % 2
            cb = n // 2
            src_w = w1 if mat == 0 else w3
            dst = w1q_cb if mat == 0 else w3q_cb
            slot = n % 2
            if n >= 2:
                sy.wait(Ste, wt_te[n - 2])
            sy.op("dma_start", out=wtb[:, slot * D:(slot + 1) * D],
                  in_=src_w[cb * 128:(cb + 1) * 128, :]).inc(Dwl[slot])
            te.wait(Dwl[slot])
            te.wait(Dcn)
            b = take()
            for dt_i in range(DT):
                wt_te[n] = te.op("transpose", PS[b][:, dt_i * 128:(dt_i + 1) * 128],
                                 wtb[:, slot * D + dt_i * 128: slot * D + (dt_i + 1) * 128],
                                 cI[:]).inc(Ste)
            ve.wait(Ste, wt_te[n])
            ve.wait(Dww[slot])
            wt_ev[n] = ve.op("tensor_copy", out=wtev[:, slot * D:(slot + 1) * D],
                             in_=PS[b][:, 0:D]).inc(Sve)
            ps_rel[b] = [(Sve, wt_ev[n])]
            sc.wait(Sve, wt_ev[n])
            sc.op("dma_start", out=dst[cb], in_=wtev[:, slot * D:(slot + 1) * D]).inc(Dww[slot])
            wt_mark[n] = [(s_, s_.n) for s_ in Dww]

    sy.wait(Dco)
    for d1 in range(DT):
        sy.op("dma_start", out=wotb[:, d1 * D:(d1 + 1) * D],
              in_=s_wo[:, d1 * 128:(d1 + 1) * 128], transpose=True).inc(Dwot)

    te.wait(Dwot)
    te.wait(Dwv)
    wvo_te = [0] * DT
    for d2t in range(DT):
        b = take()
        for d1 in range(DT):
            lhs = wotb[:, d1 * D + d2t * 128: d1 * D + (d2t + 1) * 128]
            for (o, w) in _ch(D):
                hh = te.op("matmul", PS[b][:, o:o + w], lhs,
                           wvb[:, d1 * D + o:d1 * D + o + w],
                           start=(d1 == 0), stop=(d1 == DT - 1))
                if d1 == DT - 1:
                    wvo_te[d2t] = hh.inc(Ste)
        ve.wait(Ste, wvo_te[d2t])
        if d2t >= 1:
            ve.wait(Dvw[0])
        sve_v = ve.op("tensor_copy", out=wvoev[:, 0:D],
                      in_=PS[b][:, 0:D]).inc(Sve)
        ps_rel[b] = [(Sve, sve_v)]
        gp.wait(Sve, sve_v)
        gp.op("dma_start", out=s_wvoT[d2t * 128:(d2t + 1) * 128, :],
              in_=wvoev[:, 0:D]).inc(Dvw[0])
        pump_s1(max(1, (HT + DT - 1) // DT))
    pump_s1(HT)
    p3_te_end = Ste.n
    p3_ve_end = Sve.n

    p2v_release = [(Ste, p3_te_end), (Sve, p3_ve_end),
                   (Dvw[0], Dvw[0].n), (Dvw[1], Dvw[1].n), (Dwv, Dwv.n)]
    p2v.close()

    p46 = contextlib.ExitStack()
    xnT = p46.enter_context(nc.sbuf_tensor("xnT", [128, DT * T], BF16, is_transpose_dest=True))
    wqkT = p46.enter_context(nc.sbuf_tensor("wqkT", [128, 2 * DT * R], BF16, is_transpose_dest=True))
    qfT = p46.enter_context(nc.sbuf_tensor("qfT", [R, T], BF16))
    kfT = p46.enter_context(nc.sbuf_tensor("kfT", [R, T], BF16))

    boundary(p2v_release + p1_release)

    sy.wait(Dxw[0])
    sy.wait(Dxw[1])
    for d in range(DT):
        sy.op("dma_start", out=xnT[:, d * T:(d + 1) * T],
              in_=s_xn[:, d * 128:(d + 1) * 128], transpose=True).inc(DxnT)
    sy.wait(Dcq)
    sy.wait(Dck)
    for d in range(DT):
        sy.op("dma_start", out=wqkT[:, d * R:(d + 1) * R],
              in_=s_wq[:, d * 128:(d + 1) * 128], transpose=True).inc(Dwqk)
        sy.op("dma_start", out=wqkT[:, (DT + d) * R:(DT + d + 1) * R],
              in_=s_wk[:, d * 128:(d + 1) * 128], transpose=True).inc(Dwqk)
    wvob = p46.enter_context(nc.sbuf_tensor("wvob", [128, DT * D], BF16, is_transpose_dest=True))
    sy.wait(Dvw[0])
    sy.wait(Dvw[1])
    for d in range(DT):
        sy.op("dma_start", out=wvob[:, d * D:(d + 1) * D],
              in_=s_wvoT[:, d * 128:(d + 1) * 128], transpose=True).inc(Dwvo)

    te.wait(DxnT)
    te.wait(Dwqk)
    for (o, w) in _ch(T):
        b0 = take()
        b1 = take()
        for d in range(DT):
            h0 = te.op("matmul", PS[b0][0:R, o:o + w], wqkT[:, d * R:(d + 1) * R],
                       xnT[:, d * T + o:d * T + o + w], start=(d == 0), stop=(d == DT - 1))
        h0.inc(Ste)
        for d in range(DT):
            h1 = te.op("matmul", PS[b1][0:R, o:o + w], wqkT[:, (DT + d) * R:(DT + d + 1) * R],
                       xnT[:, d * T + o:d * T + o + w], start=(d == 0), stop=(d == DT - 1))
        qk_te = h1.inc(Ste)
        sc.wait(Ste, qk_te)
        sc.op("activation", out=qfT[:, o:o + w], in_=PS[b0][0:R, o:o + w], func=AF.Copy).inc(Ssc)
        qk_sc = sc.op("activation", out=kfT[:, o:o + w], in_=PS[b1][0:R, o:o + w],
                      func=AF.Copy).inc(Ssc)
        ps_rel[b0] = [(Ssc, qk_sc)]
        ps_rel[b1] = [(Ssc, qk_sc)]
        pump_s2(4)
        pump_wt(4)
    qk_done_sc = Ssc.n

    abuf = p46.enter_context(nc.sbuf_tensor("abuf", [128, 2 * T], BF16))
    negm = p46.enter_context(nc.sbuf_tensor("negm", [128, 2], F32))
    sumb = p46.enter_context(nc.sbuf_tensor("sumb", [128, 2], F32))

    te.wait(Ssc, qk_done_sc)
    a_sc = [0] * TT
    for i in range(TT):
        bs = i % 2
        b = take()
        s_te = 0
        for (o, w) in _ch(T):
            s_te = te.op("matmul", PS[b][:, o:o + w], qfT[:, i * 128:(i + 1) * 128],
                         kfT[:, o:o + w], start=True, stop=True).inc(Ste)
        ve.wait(Ste, s_te)
        if i >= 2:
            ve.wait(Ssc, a_sc[i - 2])
        negm_v = ve.op("reduce_max", out=negm[:, bs:bs + 1], in_=PS[b][:, 0:T], axis=AXX,
                       negate=True).inc(Sve)
        sc.wait(Sve, negm_v)
        sc.wait(Daw[bs])
        a_sc[i] = sc.op("activation", out=abuf[:, bs * T:(bs + 1) * T], in_=PS[b][:, 0:T],
                        func=AF.Exp, bias=negm[:, bs:bs + 1]).inc(Ssc)
        ps_rel[b] = [(Ssc, a_sc[i]), (Sve, negm_v)]
        ve.wait(Ssc, a_sc[i])
        ve.op("reduce_sum", out=sumb[:, bs:bs + 1], in_=abuf[:, bs * T:(bs + 1) * T],
              axis=AXX).inc(Sve)
        ve.wait(Sve)
        ve.op("reciprocal", out=recipb[:, i:i + 1], in_=sumb[:, bs:bs + 1]).inc(Sve)
        gp.wait(Ssc, a_sc[i])
        gp.op("dma_start", out=s_a[i * 128:(i + 1) * 128, :],
              in_=abuf[:, bs * T:(bs + 1) * T]).inc(Daw[bs])
        pump_s2(2)
        pump_wt(6)
    recip_done = Sve.n

    uev = p46.enter_context(nc.sbuf_tensor("uev", [128, 2 * D], BF16))
    te.wait(Dwvo)
    for i in range(TT):
        bs = i % 2
        b = take()
        u_te = 0
        for d in range(DT):
            lhs = xnT[:, d * T + i * 128: d * T + (i + 1) * 128]
            for (o, w) in _ch(D):
                hh = te.op("matmul", PS[b][:, o:o + w], lhs,
                           wvob[:, d * D + o:d * D + o + w],
                           start=(d == 0), stop=(d == DT - 1))
                if d == DT - 1:
                    u_te = hh.inc(Ste)
        sc.wait(Ste, u_te)
        sc.wait(Duw[bs])
        ssc_v = sc.op("activation", out=uev[:, bs * D:(bs + 1) * D], in_=PS[b][:, 0:D],
                      func=AF.Copy).inc(Ssc)
        ps_rel[b] = [(Ssc, ssc_v)]
        gp.wait(Ssc, ssc_v)
        gp.op("dma_start", out=s_u[i * 128:(i + 1) * 128, :],
              in_=uev[:, bs * D:(bs + 1) * D]).inc(Duw[bs])
        pump_s2(2)
    pump_s2(HT)
    assert s2["n"] == HT and len(w2h_mark) == NGRP

    pump_wt(NWT)
    assert wt_state["n"] == NWT

    p46_te_end = Ste.n
    p46.close()
    pfold_release = [(Ste, max(max(st2_te), 1)),
                     *[(s_, s_.n) for s_ in Dw2u], *[(s_, s_.n) for s_ in Dw2hw],
                     *[(s_, s_.n) for s_ in Dwl], *[(s_, s_.n) for s_ in Dww]]
    pfold.close()
    p4t.close()

    p7 = contextlib.ExitStack()
    xnTh = p7.enter_context(nc.sbuf_tensor("xnTh", [128, DT * TH], BF16, is_transpose_dest=True))
    acch = p7.enter_context(nc.sbuf_tensor("acch", [128, THT * D], BF16))
    gbuf = p7.enter_context(nc.sbuf_tensor("gbuf", [128, GCT * TH], BF16))
    w1b = p7.enter_context(nc.sbuf_tensor("w1b", [128, 2 * DT * SCW], BF16))
    w3b = p7.enter_context(nc.sbuf_tensor("w3b", [128, 2 * DT * SCW], BF16))
    w2hb = p7.enter_context(nc.sbuf_tensor("w2hb", [128, GCT * D], BF16))
    silt = p7.enter_context(nc.sbuf_tensor("silt", [128, 2 * TH], BF16))

    boundary([(Ste, p46_te_end), (Sve, Sve.n), (Ssc, Ssc.n),
              (Daw[0], Daw[0].n), (Daw[1], Daw[1].n),
              (Duw[0], Duw[0].n), (Duw[1], Duw[1].n),
              *pfold_release])

    sil_guard = [0, 0]
    acc_spill = [0, 0]
    for half in range(2):
        r0h = half * TH
        for d in range(DT):
            sy.op("dma_start", out=xnTh[:, d * TH:(d + 1) * TH],
                  in_=s_xn[r0h:r0h + TH, d * 128:(d + 1) * 128], transpose=True).inc(DxnTh)
        xnTh_done = DxnTh.n

        sup_te = [0] * NSUP

        def p7_load_super(s, _sup_te=sup_te):
            if s >= NSUP:
                return
            bsl = s % 2
            mark = wt_mark[2 * ((s + 1) * SCT - 1) + 1]
            for s_, v in mark:
                sy.wait(s_, v)
                gp.wait(s_, v)
            if s >= 2:
                sy.wait(Ste, _sup_te[s - 2])
                gp.wait(Ste, _sup_te[s - 2])
            for ct in range(SCT):
                cb = s * SCT + ct
                sy.op("dma_start",
                      out=w1b[:, bsl * DT * SCW + ct * DT * 128:bsl * DT * SCW + (ct + 1) * DT * 128],
                      in_=w1q_cb[cb]).inc(Dw1[bsl])
                gp.op("dma_start",
                      out=w3b[:, bsl * DT * SCW + ct * DT * 128:bsl * DT * SCW + (ct + 1) * DT * 128],
                      in_=w3q_cb[cb]).inc(Dw3[bsl])

        p7_load_super(0)
        p7_load_super(1)

        te.wait(DxnTh, xnTh_done)
        acc_ve = [0] * THT
        for g in range(NGRP):
            gp.wait(Ste)
            for s_, v in w2h_mark[g]:
                gp.wait(s_, v)
            gp.op("dma_start", out=w2hb[:, 0:GCT * D],
                  in_=w2h_rows[:, g * GCT:(g + 1) * GCT, :]).inc(Dw2hb)

            g_mul = 0
            for si in range(SUPG):
                s = g * SUPG + si
                bsl = s % 2
                te.wait(Dw1[bsl])
                te.wait(Dw3[bsl])
                for ct in range(SCT):
                    c_glob = s * SCT + ct
                    cg = c_glob - g * GCT
                    slot = c_glob % 2
                    pb = take()
                    gate_te = lin_te = 0
                    for d in range(DT):
                        lhs = w1b[:, bsl * DT * SCW + ct * DT * 128 + d * 128:bsl * DT * SCW + ct * DT * 128 + (d + 1) * 128]
                        for (o, w) in _ch(TH):
                            hh = te.op("matmul", PS[pb][:, o:o + w], lhs,
                                       xnTh[:, d * TH + o:d * TH + o + w],
                                       start=(d == 0), stop=(d == DT - 1))
                            if d == DT - 1:
                                gate_te = hh.inc(Ste)
                    for d in range(DT):
                        lhs = w3b[:, bsl * DT * SCW + ct * DT * 128 + d * 128:bsl * DT * SCW + ct * DT * 128 + (d + 1) * 128]
                        for (o, w) in _ch(TH):
                            hh = te.op("matmul", PS[pb][:, 1024 + o:1024 + o + w], lhs,
                                       xnTh[:, d * TH + o:d * TH + o + w],
                                       start=(d == 0), stop=(d == DT - 1))
                            if d == DT - 1:
                                lin_te = hh.inc(Ste)
                    sc.wait(Ste, gate_te)
                    sc.wait(Sve, sil_guard[slot])
                    sil_sc = sc.op("activation", out=silt[:, slot * TH:(slot + 1) * TH],
                                   in_=PS[pb][:, 0:TH], func=AF.Silu).inc(Ssc)
                    ve.wait(Ssc, sil_sc)
                    ve.wait(Ste, lin_te)
                    g_mul = ve.op("tensor_tensor", out=gbuf[:, cg * TH:(cg + 1) * TH],
                                  in0=silt[:, slot * TH:(slot + 1) * TH],
                                  in1=PS[pb][:, 1024:1024 + TH], op=ALU.mult).inc(Sve)
                    sil_guard[slot] = g_mul
                    ps_rel[pb] = [(Sve, g_mul), (Ssc, sil_sc)]
                sup_te[s] = Ste.n
                p7_load_super(s + 2)
            te.wait(Sve, g_mul)
            te.wait(Dw2hb)
            for j in range(THT):
                pb = take()
                dn_te = 0
                for ci in range(GCT):
                    lhs = gbuf[:, ci * TH + j * 128: ci * TH + (j + 1) * 128]
                    for (o, w) in _ch(D):
                        hh = te.op("matmul", PS[pb][:, o:o + w], lhs,
                                   w2hb[:, ci * D + o:ci * D + o + w],
                                   start=(ci == 0), stop=(ci == GCT - 1))
                        if ci == GCT - 1:
                            dn_te = hh.inc(Ste)
                ve.wait(Ste, dn_te)
                if g == 0:
                    if half == 1 and j == 0:
                        ve.wait(Daccw, acc_spill[0])
                    hh = ve.op("tensor_copy", out=acch[:, j * D:(j + 1) * D], in_=PS[pb][:, 0:D])
                else:
                    ve.wait(Sve, acc_ve[j])
                    hh = ve.op("tensor_tensor", out=acch[:, j * D:(j + 1) * D],
                               in0=PS[pb][:, 0:D], in1=acch[:, j * D:(j + 1) * D], op=ALU.add)
                acc_ve[j] = hh.inc(Sve)
                ps_rel[pb] = [(Sve, acc_ve[j])]
            if g + 1 < NGRP:
                ve.wait(Ste, Ste.n)
        for j in range(THT):
            gp.wait(Sve, acc_ve[j])
        gp.op("dma_start", out=acc_rows[:, half * THT:(half + 1) * THT, :],
              in_=acch[:, 0:THT * D]).inc(Daccw)
        acc_spill[half] = Daccw.n
        if half == 0:
            sy.wait(Ste, Ste.n)
            ve.wait(Ste, Ste.n)
    p7_te_end = Ste.n
    p7.close()

    p8 = contextlib.ExitStack()
    ubuf = p8.enter_context(nc.sbuf_tensor("ubuf", [128, DT * D], BF16))
    NQC = max(1, T // 512)
    QW = T // NQC
    QT = QW // 128
    atb = p8.enter_context(nc.sbuf_tensor("atb", [128, 2 * TT * QW], BF16, is_transpose_dest=True))
    xrb = p8.enter_context(nc.sbuf_tensor("xrb", [128, 2 * D], F32))
    accrb = p8.enter_context(nc.sbuf_tensor("accrb", [128, 2 * D], BF16))
    t2b = p8.enter_context(nc.sbuf_tensor("t2b", [128, 2 * D], F32))
    outb = p8.enter_context(nc.sbuf_tensor("outb", [128, 2 * D], F32))

    boundary([(Ste, p7_te_end), (Sve, Sve.n), (Ssc, Ssc.n), (Daccw, Daccw.n)])
    gp.wait(Duw[0])
    gp.wait(Duw[1])
    for tk in range(TT):
        gp.op("dma_start", out=ubuf[:, tk * D:(tk + 1) * D],
              in_=s_u[tk * 128:(tk + 1) * 128, :]).inc(Dub)

    at_te = [0] * NQC
    sy.wait(Daw[0])
    sy.wait(Daw[1])
    ve.wait(Sve, recip_done)

    def p8_load_at(q):
        if q >= NQC:
            return
        bq = q % 2
        if q >= 2:
            sy.wait(Ste, at_te[q - 2])
        for tk in range(TT):
            sy.op("dma_start", out=atb[:, (bq * TT + tk) * QW:(bq * TT + tk + 1) * QW],
                  in_=s_a[q * QW:(q + 1) * QW, tk * 128:(tk + 1) * 128],
                  transpose=True).inc(Dat[bq])

    p8_load_at(0)
    p8_load_at(1)
    te.wait(Dub)
    out_ve = [0] * TT

    def p8_load_xacc(i):
        if i >= TT:
            return
        bs = i % 2
        if i >= 2:
            sc.wait(Sve, out_ve[i - 2])
        sc.op("dma_start", out=xrb[:, bs * D:(bs + 1) * D],
              in_=x[i * 128:(i + 1) * 128, :]).inc(Dxa[bs])
        if i == 0:
            sc.wait(Daccw)
        sc.op("dma_start", out=accrb[:, bs * D:(bs + 1) * D],
              in_=s_acc[i * 128:(i + 1) * 128, :]).inc(Dxa[bs])

    p8_load_xacc(0)
    p8_load_xacc(1)
    for q in range(NQC):
        bq = q % 2
        te.wait(Dat[bq])
        for j in range(QT):
            i = q * QT + j
            bs = i % 2
            b = take()
            a_te = 0
            for tk in range(TT):
                lhs = atb[:, (bq * TT + tk) * QW + j * 128:(bq * TT + tk) * QW + (j + 1) * 128]
                for (o, w) in _ch(D):
                    hh = te.op("matmul", PS[b][:, o:o + w], lhs,
                               ubuf[:, tk * D + o:tk * D + o + w],
                               start=(tk == 0), stop=(tk == TT - 1))
                    if tk == TT - 1:
                        a_te = hh.inc(Ste)
            if j == QT - 1:
                at_te[q] = a_te
            ve.wait(Ste, a_te)
            ve.wait(Dxa[bs])
            ve.wait(Dow[bs])
            ve.op("tensor_tensor", out=t2b[:, bs * D:(bs + 1) * D],
                  in0=xrb[:, bs * D:(bs + 1) * D], in1=accrb[:, bs * D:(bs + 1) * D],
                  op=ALU.add).inc(Sve)
            ve.wait(Sve)
            out_ve[i] = ve.op("scalar_tensor_tensor", out=outb[:, bs * D:(bs + 1) * D],
                              in0=PS[b][:, 0:D], scalar=recipb[:, i:i + 1],
                              in1=t2b[:, bs * D:(bs + 1) * D],
                              op0=ALU.mult, op1=ALU.add).inc(Sve)
            ps_rel[b] = [(Sve, out_ve[i])]
            gp.wait(Sve, out_ve[i])
            gp.op("dma_start", out=out[i * 128:(i + 1) * 128, :],
                  in_=outb[:, bs * D:(bs + 1) * D]).inc(Dow[bs])
            p8_load_xacc(i + 2)
        p8_load_at(q + 2)
    gp.wait(Dow[0])
    gp.wait(Dow[1])
    p8.close()

    with nc.Block() as block:
        @block.sync
        def _(eng):
            sy.play(eng)

        @block.scalar
        def _(eng):
            sc.play(eng)

        @block.vector
        def _(eng):
            ve.play(eng)

        @block.tensor
        def _(eng):
            te.play(eng)

        @block.gpsimd
        def _(eng):
            gp.play(eng)

    stack.close()
    return nc


def kernel(L0_attn_hidden_states, L0_attn_Wq, L0_attn_Wk, L0_attn_Wv, L0_attn_Wo,
           L0_ffn_W1, L0_ffn_W2, L0_ffn_W3):
    x = np.ascontiguousarray(np.asarray(L0_attn_hidden_states, dtype=np.float32))
    B, T, D = x.shape
    H = L0_ffn_W1.shape[0]
    R = L0_attn_Wq.shape[0]
    nc = build_program(T, D, H, R)
    h128, bd = make_consts(H)
    common = {
        "wq": np.ascontiguousarray(np.asarray(L0_attn_Wq, np.float32)),
        "wk": np.ascontiguousarray(np.asarray(L0_attn_Wk, np.float32)),
        "wv": np.ascontiguousarray(np.asarray(L0_attn_Wv, np.float32)),
        "wo": np.ascontiguousarray(np.asarray(L0_attn_Wo, np.float32)),
        "w1": np.ascontiguousarray(np.asarray(L0_ffn_W1, np.float32)),
        "w2": np.ascontiguousarray(np.asarray(L0_ffn_W2, np.float32)),
        "w3": np.ascontiguousarray(np.asarray(L0_ffn_W3, np.float32)),
        "h128": h128,
        "bd": bd,
        "fconst": np.tile(np.array([[EPS, 1.0 / np.sqrt(H)]], np.float32), (128, 1)),
        "identc": np.eye(128, dtype=np.float32),
    }
    in_maps = [{"x": np.ascontiguousarray(x[i]), **common} for i in range(B)]
    res = run_bass_kernel_spmd(nc, in_maps, core_ids=list(range(B)))
    return np.stack([np.asarray(r["out"], dtype=np.float32) for r in res.results], axis=0)
